# version 02 (final)
# baseline (speedup 1.0000x reference)
import math
from contextlib import ExitStack

import numpy as np
import concourse.bass as bass
import concourse.mybir as mybir
from concourse.bass_utils import run_bass_kernel_spmd

F32 = mybir.dt.float32
BF16 = mybir.dt.bfloat16
AF = mybir.ActivationFunctionType
ALU = mybir.AluOpType
AX = mybir.AxisListType

NCORES = 8
T = 1024
SEQ = 2048
D = 2048
KC = 16
HID = 8192
HC = 64
NH = 16
MYH = 8
HD = 128
EPS = 1e-5

ALL_BLOCKS = [0, 1, 2, 3, 4, 5, 6, 7]
LAUNCH_PLAN = [ALL_BLOCKS]
DBG_G = None
DBG_AV = None

ENGS = ("tensor", "vector", "scalar", "gpsimd", "sync")
ESEM = {"tensor": "pe", "vector": "dve", "scalar": "act", "gpsimd": "pool",
        "sync": "sp"}
DMASEMS = ("dwa", "dwb", "dx")
CCSEM = "dcc"


PARANOID = False


class Planner:
    def __init__(self):
        self.ops = {e: [] for e in ENGS}
        self.semv = {}
        for s in list(ESEM.values()) + list(DMASEMS) + [CCSEM]:
            self.semv[s] = 0

    def add(self, eng, fn, waits=(), inc=None, tag=""):
        w = {}
        if PARANOID:
            for s, v in self.semv.items():
                if v > 0 and s != ESEM[eng]:
                    w[s] = v
        for tk in waits:
            if tk is None:
                continue
            s, v = tk
            w[s] = max(w.get(s, 0), v)
        tok = None
        amt = 0
        sem = None
        if inc is True:
            sem, amt = ESEM[eng], 1
        elif inc == CCSEM:
            sem, amt = inc, 1
        elif inc:
            sem, amt = inc, 16
        if sem is not None:
            self.semv[sem] += amt
            tok = (sem, self.semv[sem])
        self.ops[eng].append((fn, sorted(w.items()), (sem, amt), tag))
        return tok


class Ctx:
    def __init__(self, nc, plan):
        self.nc = nc
        self.p = plan
        self.t = {}
        self.uid = 0
        self.ps_readers = {0: [], 1: [], 2: [], 3: []}
        self.bufr = {}
        self.resid_ready = None
        self.ident_tok = None
        self.tri_tok = None
        self.roff_reg = None
        self.roff2_reg = None

    def nuid(self):
        self.uid += 1
        return self.uid

    def ps(self, pair, c0=0, c1=1024):
        return self.t["PS"][:, pair * 1024 + c0: pair * 1024 + c1]

    def ps_wait(self, pair):
        lst = self.ps_readers[pair]
        self.ps_readers[pair] = []
        return lst

    def ps_read(self, pair, tok):
        if tok:
            self.ps_readers[pair].append(tok)

    def buf_wait(self, name):
        lst = self.bufr.get(name, [])
        self.bufr[name] = []
        return lst

    def buf_read(self, name, tok):
        if tok:
            self.bufr.setdefault(name, []).append(tok)



def host_prep(inp, blocks):
    pre = {}
    f32 = lambda a: np.asarray(a, dtype=np.float32)

    def rep(name, arr):
        pre[name] = [np.ascontiguousarray(arr)] * NCORES

    if 0 in blocks:
        win = f32(inp["L0_attn_a_W_in"])
        wout = f32(inp["L0_attn_a_W_out"])
        rep("w_a0_in", win.reshape(48, 128, KC, 128).transpose(0, 3, 2, 1)
            .reshape(48, 128, D))
        rep("w_a0_out", wout.reshape(KC, 128, KC, 128).transpose(0, 3, 2, 1)
            .reshape(KC, 128, D))
        bias = np.zeros((128, 64), np.float32)
        bias[:, 0:48] = f32(inp["L0_attn_a_b_in"]).reshape(48, 128).T
        bias[:, 48:64] = f32(inp["L0_attn_a_b_out"]).reshape(KC, 128).T
        rep("b_a0", bias)
        shuf = np.zeros((128, 128), np.float32)
        for k in range(64):
            shuf[2 * k + 1, 2 * k] = -1.0
            shuf[2 * k, 2 * k + 1] = 1.0
        rep("w_a0_shuf", shuf)
        inv = (10000.0 ** (-(np.arange(0, 1024, 2, dtype=np.float64) / 1024.0)))
        fidx = np.arange(1024) // 2
        trigs = []
        for c in range(NCORES):
            pos = (c % 2) * T + np.arange(T, dtype=np.float64)
            ang = pos[None, :] * inv[fidx][:, None]
            tab = np.stack([np.cos(ang), np.sin(ang)], axis=0)
            trigs.append(np.ascontiguousarray(
                tab.reshape(2, 8, 128, T).astype(np.float32)))
        pre["trig_a0"] = trigs

    if any(b in blocks for b in ATTN_BLOCKS):
        rep("ident", np.eye(128, dtype=np.float32))
        rep("tri", np.tril(np.ones((128, 128), np.float32)))
        pre["coremeta"] = [np.array([[(c % 2) * T, 0]], np.int32)
                           for c in range(NCORES)]

    if 2 in blocks:
        Ws = {w: f32(inp[f"L0_attn_b_W{w}"]) for w in "qkvo"}
        Bs = {w: f32(inp[f"L0_attn_b_B{w}"]) for w in "qkvo"}
        As = {w: f32(inp[f"L0_attn_b_A{w}"]) for w in "qkvo"}
        bs = {w: f32(inp[f"L0_attn_b_b{w}"]) for w in "qkvo"}
        qkv = np.zeros((NH, 3, 128, SLOTW), np.float32)
        for wi, w in enumerate("qkv"):
            sl = (Ws[w].reshape(NH, 128, KC, 128).transpose(0, 3, 2, 1)
                  .reshape(NH, 128, D))
            qkv[:, wi, :, 0:D] = sl
            at = As[w].T
            for gh in range(NH):
                qkv[gh, wi, wi * 32:wi * 32 + 16, D:D + 128] = \
                    at[:, gh * 128:(gh + 1) * 128]
        pre["w_a2_qkv"] = [np.ascontiguousarray(qkv[(c % 2) * 8:
                                                    (c % 2) * 8 + 8])
                           for c in range(NCORES)]
        wo2 = np.zeros((KC, 128, SLOTW), np.float32)
        wo2[:, :, 0:D] = (Ws["o"].reshape(KC, 128, KC, 128)
                          .transpose(0, 3, 2, 1).reshape(KC, 128, D))
        ao = As["o"].T
        for ft in range(KC):
            wo2[ft, 0:16, D:D + 128] = ao[:, ft * 128:(ft + 1) * 128]
        rep("w_a2_o", wo2)
        bt = np.zeros((128, 1024), np.float32)
        for wi, w in enumerate("qkv"):
            bT = Bs[w].T
            for kc in range(KC):
                bt[:, (wi * KC + kc) * 16:(wi * KC + kc) * 16 + 16] = \
                    bT[kc * 128:(kc + 1) * 128, :]
        boT = Bs["o"].T
        for kc in range(KC):
            bt[:, 768 + kc * 16:768 + kc * 16 + 16] = \
                boT[kc * 128:(kc + 1) * 128, :]
        rep("w_a2_B", bt)
        b2s = []
        for c in range(NCORES):
            rank = c % 2
            bias = np.zeros((128, 64), np.float32)
            for wi, w in enumerate("qkv"):
                for h in range(MYH):
                    g = rank * 8 + h
                    bias[:, wi * 8 + h] = bs[w][g * 128:(g + 1) * 128]
            bias[:, 48:64] = bs["o"].reshape(KC, 128).T
            b2s.append(np.ascontiguousarray(bias))
        pre["b_a2"] = b2s

    if 4 in blocks:
        wqkv = f32(inp["L1_attn_a_Wqkv"])
        bqkv = f32(inp["L1_attn_a_bqkv"])
        wo = f32(inp["L1_attn_a_Wo"])
        bo = f32(inp["L1_attn_a_bo"])
        w5 = (wqkv.reshape(KC, 128, 3, NH, 128).transpose(3, 2, 1, 0, 4)
              .reshape(NH, 3, 128, D))
        wos = (wo.reshape(KC, 128, KC, 128).transpose(2, 1, 0, 3)
               .reshape(KC, 128, D))
        pre["w_a4_qkv"] = [np.ascontiguousarray(w5[(c % 2) * 8:(c % 2) * 8 + 8])
                           for c in range(NCORES)]
        rep("w_a4_o", wos)
        b4s = []
        for c in range(NCORES):
            rank = c % 2
            bias = np.zeros((128, 64), np.float32)
            for w in range(3):
                for h in range(MYH):
                    g = rank * 8 + h
                    bias[:, w * 8 + h] = bqkv[w * 2048 + g * 128:
                                              w * 2048 + (g + 1) * 128]
            bias[:, 48:64] = bo.reshape(KC, 128).T
            b4s.append(np.ascontiguousarray(bias))
        pre["b_a4"] = b4s

    if 6 in blocks:
        def qkv_slabs(arr):
            return (arr.reshape(KC, 128, NH, 128).transpose(2, 1, 0, 3)
                    .reshape(NH, 128, D))

        cslab = np.stack([qkv_slabs(f32(inp[f"L1_attn_b_w{w}_codes"]))
                          for w in "qkv"], axis=1)
        sslab = np.stack(
            [qkv_slabs(np.repeat(f32(inp[f"L1_attn_b_w{w}_scales"]), 32,
                                 axis=0)) for w in "qkv"], axis=1)
        pre["w_a6_qkv_c"] = [np.ascontiguousarray(
            cslab[(c % 2) * 8:(c % 2) * 8 + 8]) for c in range(NCORES)]
        pre["w_a6_qkv_s"] = [np.ascontiguousarray(
            sslab[(c % 2) * 8:(c % 2) * 8 + 8]) for c in range(NCORES)]
        woc = f32(inp["L1_attn_b_wo_codes"])
        wosc = np.repeat(f32(inp["L1_attn_b_wo_scales"]), 32, axis=0)
        tr = lambda a: (a.reshape(KC, 128, KC, 128).transpose(2, 1, 0, 3)
                        .reshape(KC, 128, D))
        rep("w_a6_o_c", tr(woc))
        rep("w_a6_o_s", tr(wosc))
        b6s = []
        for c in range(NCORES):
            rank = c % 2
            bias = np.zeros((128, 64), np.float32)
            for wi, w in enumerate("qkv"):
                bb = f32(inp[f"L1_attn_b_b{w}"])
                for h in range(MYH):
                    g = rank * 8 + h
                    bias[:, wi * 8 + h] = bb[g * 128:(g + 1) * 128]
            bias[:, 48:64] = f32(inp["L1_attn_b_bo"]).reshape(KC, 128).T
            b6s.append(np.ascontiguousarray(bias))
        pre["b_a6"] = b6s

    if 1 in blocks or 3 in blocks:
        wg = f32(inp["L0_ffn_w_gate"])
        wu = f32(inp["L0_ffn_w_up"])
        wd = f32(inp["L0_ffn_w_down"])
        rep("w_f0_gate", wg.reshape(HC, 128, KC, 128).transpose(0, 3, 2, 1)
            .reshape(HC, 128, D))
        rep("w_f0_up", wu.reshape(HC, 128, KC, 128).transpose(0, 3, 2, 1)
            .reshape(HC, 128, D))
        rep("w_f0_down", wd.reshape(KC, 128, HC, 128).transpose(0, 2, 3, 1)
            .reshape(KC, 4, 16, 128, 128).transpose(0, 1, 3, 2, 4)
            .reshape(KC, 4, 128, 16 * 128))

    if 5 in blocks or 7 in blocks:
        wg = f32(inp["L1_ffn_W_gate"])
        wu = f32(inp["L1_ffn_W_up"])
        wd = f32(inp["L1_ffn_W_down"])
        rep("w_f1_gate", wg.reshape(HC, 128, KC, 128).transpose(0, 3, 2, 1)
            .reshape(HC, 128, D))
        rep("w_f1_up", wu.reshape(HC, 128, KC, 128).transpose(0, 3, 2, 1)
            .reshape(HC, 128, D))
        rep("w_f1_down", wd.reshape(KC, 128, HC, 128).transpose(0, 2, 3, 1)
            .reshape(KC, 4, 16, 128, 128).transpose(0, 1, 3, 2, 4)
            .reshape(KC, 4, 128, 16 * 128))
        v = np.zeros((128, 224), np.float32)
        v[:, 0:16] = f32(inp["L1_ffn_SU_ug"]).reshape(16, 128).T
        v[:, 16:80] = f32(inp["L1_ffn_SU_dp"]).reshape(64, 128).T
        wsug = f32(inp["L1_ffn_Wscale_ug"])
        v[:, 80:144] = wsug[:8192].reshape(64, 128).T
        v[:, 144:208] = wsug[8192:].reshape(64, 128).T
        v[:, 208:224] = f32(inp["L1_ffn_Wscale_dp"]).reshape(16, 128).T
        rep("v_f1", v)
        ii, jj = np.meshgrid(np.arange(128), np.arange(128), indexing="ij")
        h128 = np.where(
            np.bitwise_count(np.bitwise_and(ii, jj)) % 2 == 0, 1.0, -1.0)
        rep("h128", h128.astype(np.float32))
    return pre



ATTN_BLOCKS = (2, 4, 6)


def declare_ios(ctx, blocks):
    nc = ctx.nc
    names = []
    ctx.t["rin"] = nc.dram_tensor("rin", [KC, 128, T], F32, kind="ExternalInput")
    ctx.t["rout"] = nc.dram_tensor("rout", [KC, 128, T], F32, kind="ExternalOutput")

    def din(name, shape, dtype=F32):
        ctx.t[name] = nc.dram_tensor(name, shape, dtype, kind="ExternalInput")
        names.append(name)

    if any(b in blocks for b in ATTN_BLOCKS):
        din("ident", [128, 128])
        din("tri", [128, 128])
        din("coremeta", [1, 2], mybir.dt.int32)
        ctx.t["x1c"] = nc.dram_tensor("x1c", [2048, T], BF16)
        ctx.t["x1g"] = nc.dram_tensor("x1g", [4096, T], BF16)
        ctx.t["aoc"] = nc.dram_tensor("aoc", [1024, SEQ], BF16)
        ctx.t["aog"] = nc.dram_tensor("aog", [2048, SEQ], BF16)
    if 2 in blocks:
        din("w_a2_qkv", [MYH, 3, 128, SLOTW])
        din("w_a2_o", [KC, 128, SLOTW])
        din("w_a2_B", [128, 1024])
        din("b_a2", [128, 64])
    if 4 in blocks:
        din("w_a4_qkv", [MYH, 3, 128, D])
        din("w_a4_o", [KC, 128, D])
        din("b_a4", [128, 64])
    if 6 in blocks:
        din("w_a6_qkv_c", [MYH, 3, 128, D])
        din("w_a6_qkv_s", [MYH, 3, 128, D])
        din("w_a6_o_c", [KC, 128, D])
        din("w_a6_o_s", [KC, 128, D])
        din("b_a6", [128, 64])

    if 0 in blocks:
        din("w_a0_in", [48, 128, D])
        din("w_a0_out", [KC, 128, D])
        din("b_a0", [128, 64])
        din("w_a0_shuf", [128, 128])
        din("trig_a0", [2, 8, 128, T])
        ctx.t["cc_a_in"] = nc.dram_tensor("cc_a_in", [128, 16], F32)
        ctx.t["cc_a_out"] = nc.dram_tensor("cc_a_out", [128, 16], F32)
    if 1 in blocks or 3 in blocks:
        din("w_f0_gate", [HC, 128, D])
        din("w_f0_up", [HC, 128, D])
        din("w_f0_down", [KC, 4, 128, 16 * 128])
    if 5 in blocks or 7 in blocks:
        din("w_f1_gate", [HC, 128, D])
        din("w_f1_up", [HC, 128, D])
        din("w_f1_down", [KC, 4, 128, 16 * 128])
        din("v_f1", [128, 224])
        din("h128", [128, 128])
    return names


def alloc_onchip(ctx, es):
    nc = ctx.nc
    t = ctx.t

    def sb(name, shape, dtype):
        t[name] = es.enter_context(nc.sbuf_tensor(name, shape, dtype))

    def ps(name, shape, dtype):
        t[name] = es.enter_context(nc.psum_tensor(name, shape, dtype))

    sb("R", [128, KC * T], F32)
    sb("lnb", [128, KC * T], BF16)
    sb("a1b", [128, KC * T], BF16)
    sb("a2", [128, KC * T], BF16)
    sb("wst", [128, 2 * 2176], F32)
    sb("wbf", [128, 2 * 2176], BF16)
    sb("blor", [128, 1024], BF16)
    sb("sq", [128, 2048], BF16)
    sb("strips", [128, 3072], F32)
    sb("trig", [128, 2048], F32)
    sb("rsc", [128, 1024], BF16)
    sb("amine", [128, 16], F32)
    sb("afull", [128, 16], F32)
    sb("shufst", [128, 128], F32)
    sb("shufbf", [128, 128], BF16)
    sb("identbf", [128, 128], BF16)
    sb("tribf", [128, 128], BF16)
    sb("negm", [128, 4], F32)
    sb("rsn", [128, 4], F32)
    sb("fence", [128, 32], BF16)
    sb("ul", [128, 2048], BF16)
    sb("suf", [128, 224], F32)
    sb("h128bf", [128, 128], BF16)
    sb("cmsb", [1, 2], mybir.dt.int32)
    sb("bias", [128, 64], F32)
    sb("onec", [128, 1], BF16)
    sb("oner", [1, 128], F32)

    ps("PS", [128, 3072], F32)
    ps("PSB", [128, 2048], BF16)


def emit_setup(ctx):
    p, t = ctx.p, ctx.t
    tok = p.add("sync",
                lambda e: e.dma_start(
                    out=t["R"].ap().rearrange("p (k t) -> p k t", k=KC),
                    in_=t["rin"].ap().rearrange("k p t -> p k t")),
                inc="dx", tag="load_resid")
    ctx.resid_ready = tok
    t1 = p.add("gpsimd", lambda e: e.memset(t["onec"][:, :], 1.0), inc=True)
    t2 = p.add("gpsimd", lambda e: e.memset(t["oner"][:, :], 1.0), inc=True)
    ctx.init_done = t2


def emit_attn_consts(ctx):
    p, t = ctx.p, ctx.t
    cm = p.add("sync", lambda e: e.dma_start(out=t["cmsb"][:, :],
                                             in_=t["coremeta"].ap()),
               inc="dx", tag="cm_dma")
    p.add("sync", lambda e: e.reg_load(ctx.roff_reg, t["cmsb"][0:1, 0:1]),
          waits=[cm], tag="cm_load")
    p.add("sync", lambda e: e.reg_add(ctx.roff2_reg, ctx.roff_reg,
                                      8 * 128 * SEQ), tag="cm_add")
    i1 = p.add("sync", lambda e: e.dma_start(out=t["shufst"][:, :],
                                             in_=t["ident"].ap()),
               waits=ctx.buf_wait("shufst"), inc="dx", tag="id_dma")
    ic = p.add("gpsimd", lambda e: e.tensor_copy(t["identbf"][:, :],
                                                 t["shufst"][:, :]),
               waits=[i1], inc=True, tag="id_cast")
    ctx.buf_read("shufst", ic)
    t1 = p.add("sync", lambda e: e.dma_start(out=t["shufst"][:, :],
                                             in_=t["tri"].ap()),
               waits=ctx.buf_wait("shufst"), inc="dx", tag="tri_dma")
    tc = p.add("gpsimd", lambda e: e.tensor_copy(t["tribf"][:, :],
                                                 t["shufst"][:, :]),
               waits=[t1], inc=True, tag="tri_cast")
    ctx.buf_read("shufst", tc)
    ctx.ident_tok = ic
    ctx.tri_tok = tc


def emit_store(ctx):
    p, t = ctx.p, ctx.t
    tok = p.add("sync",
                lambda e: e.dma_start(
                    out=t["rout"].ap().rearrange("k p t -> p k t"),
                    in_=t["R"].ap().rearrange("p (k t) -> p k t", k=KC)),
                waits=[ctx.resid_ready], inc="dx", tag="store_resid")
    p.add("sync", lambda e: e.nop(), waits=[tok], tag="final_wait")


def emit_ln(ctx):
    nc, p, t = ctx.nc, ctx.p, ctx.t
    u = ctx.nuid()
    lnb, R, sq = t["lnb"], t["R"], t["sq"]
    strips = t["strips"]

    cast_tok = p.add("gpsimd",
                     lambda e: e.tensor_copy(lnb[:, :], R[:, :]),
                     waits=[ctx.resid_ready, ctx.init_done]
                           + ctx.buf_wait("lnb"),
                     inc=True, tag=f"ln{u}_cast")

    sr = strips[0:1, 0:1024]
    sm = strips[0:1, 1024:2048]
    sv = strips[0:1, 2048:3072]
    smr = sv
    s1 = t["PS"][0:1, 0:1024]
    s2 = t["PS"][0:1, 1024:2048]
    RB = ctx.ps(1)
    MRB = ctx.ps(2)

    w0 = ctx.ps_wait(0) + ctx.ps_wait(1) + ctx.ps_wait(2)
    sq_toks = []
    for c in range(KC):
        cc = c % 2
        sqw = [cast_tok]
        if c >= 2:
            sqw.append(sq_toks[c - 2][1])
        sq_tok = p.add("scalar",
                       lambda e, c=c, cc=cc: e.activation(
                           sq[:, cc * 1024:(cc + 1) * 1024],
                           lnb[:, c * T:(c + 1) * T], AF.Square),
                       waits=sqw, inc=True, tag=f"ln{u}_sq{c}")
        for h in range(2):
            p.add("tensor",
                  lambda e, c=c, h=h: e.matmul(
                      s1[:, h * 512:(h + 1) * 512], t["onec"][:, :],
                      lnb[:, c * T + h * 512:c * T + (h + 1) * 512],
                      start=(c == 0), stop=(c == KC - 1)),
                  waits=([cast_tok] + w0) if c == 0 and h == 0 else [],
                  tag=f"ln{u}_s1_{c}_{h}")
        pe_tok = None
        for h in range(2):
            pe_tok = p.add("tensor",
                           lambda e, c=c, h=h, cc=cc: e.matmul(
                               s2[:, h * 512:(h + 1) * 512], t["onec"][:, :],
                               sq[:, cc * 1024 + h * 512:cc * 1024 + (h + 1) * 512],
                               start=(c == 0), stop=(c == KC - 1)),
                           waits=[sq_tok] if h == 0 else [],
                           inc=True if h == 1 else None,
                           tag=f"ln{u}_s2_{c}_{h}")
        sq_toks.append((sq_tok, pe_tok))
    stats_done = sq_toks[-1][1]

    v1 = p.add("vector", lambda e: e.tensor_scalar_mul(sm, s1, 1.0 / D),
               waits=[stats_done], tag=f"ln{u}_m")
    p.add("vector", lambda e: e.tensor_scalar_mul(sv, s2, 1.0 / D))
    p.add("vector", lambda e: e.tensor_tensor(sr, sm, sm, ALU.mult))
    p.add("vector", lambda e: e.tensor_tensor(sv, sv, sr, ALU.subtract))
    var_tok = p.add("vector", lambda e: e.tensor_scalar_add(sv, sv, EPS),
                    inc=True, tag=f"ln{u}_var")
    ctx.ps_read(0, var_tok)
    ctx.ps_read(1, var_tok)
    sqrt_tok = p.add("scalar", lambda e: e.activation(sr, sv, AF.Sqrt),
                     waits=[var_tok], inc=True, tag=f"ln{u}_sqrt")
    rcp_tok = p.add("vector", lambda e: e.reciprocal(sr, sr),
                    waits=[sqrt_tok], inc=True, tag=f"ln{u}_rstd")
    mr_tok = p.add("vector", lambda e: e.tensor_tensor(smr, sm, sr, ALU.mult),
                   inc=True, tag=f"ln{u}_mr")

    bc_tok = None
    for h in range(2):
        p.add("tensor",
              lambda e, h=h: e.matmul(RB[:, h * 512:(h + 1) * 512],
                                      t["oner"][:, :],
                                      sr[:, h * 512:(h + 1) * 512],
                                      start=True, stop=True),
              waits=[rcp_tok, mr_tok] if h == 0 else [], tag=f"ln{u}_bcR{h}")
    for h in range(2):
        bc_tok = p.add("tensor",
                       lambda e, h=h: e.matmul(MRB[:, h * 512:(h + 1) * 512],
                                               t["oner"][:, :],
                                               smr[:, h * 512:(h + 1) * 512],
                                               start=True, stop=True),
                       inc=True if h == 1 else None, tag=f"ln{u}_bcM{h}")

    norm_tok = None
    for c in range(KC):
        p.add("vector",
              lambda e, c=c: e.tensor_tensor(lnb[:, c * T:(c + 1) * T],
                                             lnb[:, c * T:(c + 1) * T],
                                             RB, ALU.mult),
              waits=[bc_tok] if c == 0 else [], tag=f"ln{u}_n1_{c}")
        norm_tok = p.add("vector",
                         lambda e, c=c: e.tensor_tensor(
                             lnb[:, c * T:(c + 1) * T],
                             lnb[:, c * T:(c + 1) * T], MRB, ALU.subtract),
                         inc=True if c == KC - 1 else None,
                         tag=f"ln{u}_n2_{c}")
    ctx.ps_read(1, norm_tok)
    ctx.ps_read(2, norm_tok)
    return norm_tok


SLOTW = 2176


def stream_weight(ctx, dram_ap, slot, queue, tag, scale=None, width=2048):
    p, t = ctx.p, ctx.t
    c0 = slot * SLOTW
    d_tok = p.add(queue,
                  lambda e: e.dma_start(out=t["wst"][:, c0:c0 + width],
                                        in_=dram_ap),
                  waits=ctx.buf_wait(("wst", slot)),
                  inc="dwa" if queue == "sync" else "dwb", tag=f"dma_{tag}")
    if scale is None:
        fn = lambda e: e.tensor_copy(t["wbf"][:, c0:c0 + width],
                                     t["wst"][:, c0:c0 + width])
    else:
        fn = lambda e: e.tensor_scalar_mul(t["wbf"][:, c0:c0 + width],
                                           t["wst"][:, c0:c0 + width], scale)
    cast_tok = p.add("gpsimd", fn,
                     waits=[d_tok] + ctx.buf_wait(("wbf", slot)),
                     inc=True, tag=f"cast_{tag}")
    ctx.buf_read(("wst", slot), cast_tok)
    return cast_tok, c0


def emit_ffn0(ctx):
    nc, p, t = ctx.nc, ctx.p, ctx.t
    u = ctx.nuid()
    lnb = t["lnb"]
    norm_done = emit_ln(ctx)

    wg_d = t["w_f0_gate"].ap()
    wu_d = t["w_f0_up"].ap()
    wd_d = t["w_f0_down"].ap()
    wbf = t["wbf"]

    def g_ap(i, c0=0, c1=T):
        base = t["a2"] if i < 16 else t["a1b"]
        off = (i % 16) * T
        return base[:, off + c0:off + c1]

    r_tok = None
    for half in range(2):
        g_toks = []
        for i in range(32):
            hc = half * 32 + i
            cg, bg = stream_weight(ctx, wg_d[hc], 0, "sync", f"f{u}g{hc}")
            cu, bu = stream_weight(ctx, wu_d[hc], 1, "scalar", f"f{u}u{hc}")
            pg = (2 * i) % 3
            pu = (2 * i + 1) % 3
            gate_ps = ctx.ps(pg)
            up_ps = ctx.ps(pu)
            mw = [cg, norm_done] + ctx.ps_wait(pg)
            gate_tail = None
            for kc in range(KC):
                for h in range(2):
                    gate_tail = p.add(
                        "tensor",
                        lambda e, kc=kc, h=h, gate_ps=gate_ps, bg=bg: e.matmul(
                            gate_ps[:, h * 512:(h + 1) * 512],
                            wbf[:, bg + kc * 128:bg + (kc + 1) * 128],
                            lnb[:, kc * T + h * 512:kc * T + (h + 1) * 512],
                            start=(kc == 0), stop=(kc == KC - 1)),
                        waits=mw if kc == 0 and h == 0 else [],
                        inc=True if kc == KC - 1 and h == 1 else None,
                        tag=f"f{u}_gmm{hc}")
            mw = [cu] + ctx.ps_wait(pu)
            up_tail = None
            for kc in range(KC):
                for h in range(2):
                    up_tail = p.add(
                        "tensor",
                        lambda e, kc=kc, h=h, up_ps=up_ps, bu=bu: e.matmul(
                            up_ps[:, h * 512:(h + 1) * 512],
                            wbf[:, bu + kc * 128:bu + (kc + 1) * 128],
                            lnb[:, kc * T + h * 512:kc * T + (h + 1) * 512],
                            start=(kc == 0), stop=(kc == KC - 1)),
                        waits=mw if kc == 0 and h == 0 else [],
                        inc=True if kc == KC - 1 and h == 1 else None,
                        tag=f"f{u}_umm{hc}")
            ctx.buf_read(("wbf", 0), gate_tail)
            ctx.buf_read(("wbf", 1), up_tail)
            cc = i % 2
            sg = t["sq"][:, cc * 1024:(cc + 1) * 1024]
            silu_tok = p.add("scalar",
                             lambda e, sg=sg, gate_ps=gate_ps: e.activation(
                                 sg, gate_ps, AF.Silu),
                             waits=[gate_tail] + ctx.buf_wait(("sq", cc)),
                             inc=True, tag=f"f{u}_silu{hc}")
            ctx.ps_read(pg, silu_tok)
            g_tok = p.add("vector",
                          lambda e, i=i, sg=sg, up_ps=up_ps: e.tensor_tensor(
                              g_ap(i), sg, up_ps, ALU.mult),
                          waits=[up_tail, silu_tok] + ctx.buf_wait(("g", i)),
                          inc=True, tag=f"f{u}_g{hc}")
            ctx.ps_read(pu, g_tok)
            ctx.buf_read(("sq", cc), g_tok)
            g_toks.append(g_tok)
        all_g = g_toks[-1]

        if DBG_G == half:
            t["gdump"] = nc.dram_tensor("gdump", [32, 128, T], BF16,
                                        kind="ExternalOutput")
            tok1 = p.add("sync", lambda e: e.dma_start(
                out=t["gdump"].ap()[0:16].rearrange("k p t -> p k t"),
                in_=t["a2"].ap().rearrange("p (k t) -> p k t", k=16)),
                waits=[all_g], inc="dx")
            tok2 = p.add("sync", lambda e: e.dma_start(
                out=t["gdump"].ap()[16:32].rearrange("k p t -> p k t"),
                in_=t["a1b"].ap().rearrange("p (k t) -> p k t", k=16)),
                waits=[all_g], inc="dx")
            p.add("sync", lambda e: e.nop(), waits=[tok2])
            ctx.resid_ready = ctx.resid_ready
            return all_g

        for ft in range(KC):
            pd = (2 * 32 + ft) % 3
            dn_ps = ctx.ps(pd)
            mm_tail = None
            for sl in range(2):
                hgrp = half * 2 + sl
                cd, bd = stream_weight(ctx, wd_d[ft, hgrp], sl,
                                       "sync" if sl == 0 else "scalar",
                                       f"f{u}d{half}_{ft}_{sl}")
                mw = [cd, all_g] + (ctx.ps_wait(pd) if sl == 0 else [])
                for i in range(16):
                    gi = sl * 16 + i
                    for h in range(2):
                        mm_tail = p.add(
                            "tensor",
                            lambda e, i=i, h=h, gi=gi, bd=bd, dn_ps=dn_ps, sl=sl:
                            e.matmul(
                                dn_ps[:, h * 512:(h + 1) * 512],
                                wbf[:, bd + i * 128:bd + (i + 1) * 128],
                                g_ap(gi, h * 512, (h + 1) * 512),
                                start=(sl == 0 and i == 0),
                                stop=(sl == 1 and i == 15)),
                            waits=mw if i == 0 and h == 0 else [],
                            inc=True if sl == 1 and i == 15 and h == 1 else None,
                            tag=f"f{u}_dmm{half}_{ft}")
                ctx.buf_read(("wbf", sl), mm_tail)
            r_tok = p.add("vector",
                          lambda e, ft=ft, dn_ps=dn_ps: e.scalar_tensor_tensor(
                              t["R"][:, ft * T:(ft + 1) * T], dn_ps, 0.0,
                              t["R"][:, ft * T:(ft + 1) * T],
                              ALU.add, ALU.add),
                          waits=[mm_tail], inc=True, tag=f"f{u}_racc{half}_{ft}")
            ctx.ps_read(pd, r_tok)
        if half == 0:
            for i in range(32):
                ctx.buf_read(("g", i), r_tok)
    ctx.buf_read("lnb", r_tok)
    ctx.resid_ready = r_tok
    return r_tok


def emit_softmax_attn(ctx, cfg):
    nc, p, t = ctx.nc, ctx.p, ctx.t
    u = ctx.nuid()
    lnb = t["lnb"]
    norm_done = emit_ln(ctx)

    d_out = p.add("sync",
                  lambda e: e.dma_start(
                      out=t["x1c"].ap().rearrange("(k p) t -> p k t", p=128),
                      in_=lnb.ap().rearrange("p (k t) -> p k t", k=KC)),
                  waits=[norm_done], inc="dx", tag=f"s{u}_lndump")
    cc1 = p.add("gpsimd",
                lambda e: e.collective_compute(
                    "AllGather", ALU.bypass,
                    replica_groups=[[0, 1], [2, 3], [4, 5], [6, 7]],
                    ins=[t["x1c"].ap().opt()], outs=[t["x1g"].ap().opt()]),
                waits=[d_out], inc=CCSEM, tag=f"s{u}_lncc")
    rb = []
    for r in range(2):
        for half in range(2):
            arena = t["a1b"] if half == 0 else t["a2"]
            rb.append(p.add(
                "sync",
                lambda e, r=r, half=half, arena=arena: e.dma_start(
                    out=arena.ap().rearrange("p (k t) -> p k t", k=8)
                        [:, :, r * T:(r + 1) * T],
                    in_=t["x1g"].ap()
                        .rearrange("(r k p) t -> r p k t", r=2, p=128)
                        [r, :, half * 8:(half + 1) * 8, :]),
                waits=[cc1], inc="dx", tag=f"s{u}_lnrb{r}{half}"))
    lnf_ready = rb[-1]

    def lnf(kc):
        arena = t["a1b"] if kc < 8 else t["a2"]
        return arena[:, (kc % 8) * SEQ:(kc % 8) * SEQ + SEQ]

    pre_tok = None
    if cfg.get("pre_heads"):
        pre_tok = cfg["pre_heads"](lnf, lnf_ready)

    qh = lnb[:, 0:2048]
    kh = lnb[:, 2048:4096]
    vh = lnb[:, 4096:6144]
    vt = lnb[:, 6144:6144 + 16 * 129]
    attT = lnb[:, 8256:8256 + 2048]
    oh = lnb[:, 10304:10304 + 2048]
    attn = t["sq"]
    outq = lnb[:, 12352:12352 + 128]
    negm = t["negm"]
    SC = t["PS"][:, 0:2048]
    AVP = [ctx.ps(2, 0, 512), ctx.ps(2, 512, 1024)]

    vt_init = p.add("gpsimd", lambda e: e.memset(vt, 1.0),
                    waits=[d_out], inc=True, tag=f"s{u}_vt1")

    dst_map = {0: qh, 1: kh, 2: vh}
    slotc = [0]

    def next_slot():
        s = slotc[0] % 2
        q = "sync" if s == 0 else "scalar"
        slotc[0] += 1
        ctx._cur_wslot = s
        return s, q

    r_tok = None
    oh_dma = []
    for h in range(MYH):
        ev_toks = {}
        for w in range(3):
            slot, queue = next_slot()
            wtok, base = cfg["make_w"](h, w, slot, queue, f"s{u}_w{h}_{w}")
            pr01 = ctx.ps_wait(0) + ctx.ps_wait(1)
            tail = None
            closes = not cfg["has_extra_w"]
            for kc in range(KC):
                for q4 in range(4):
                    tail = p.add(
                        "tensor",
                        lambda e, kc=kc, q4=q4, base=base, closes=closes:
                        e.matmul(
                            SC[:, q4 * 512:(q4 + 1) * 512],
                            t["wbf"][:, base + kc * 128:base + (kc + 1) * 128],
                            lnf(kc)[:, q4 * 512:(q4 + 1) * 512],
                            start=(kc == 0), stop=(closes and kc == KC - 1)),
                        waits=([wtok, lnf_ready, pre_tok] + pr01)
                              if kc == 0 and q4 == 0 else [],
                        inc=True if closes and kc == KC - 1 and q4 == 3
                        else None,
                        tag=f"s{u}_p{h}_{w}")
            etok = cfg["extra_mm"](h, w, SC, f"s{u}_x{h}_{w}")
            ctx.buf_read(("wbf", ctx._cur_wslot), etok or tail)
            bias = cfg["bias_col"](w, h)
            dst = dst_map[w]
            ev = p.add("scalar",
                       lambda e, dst=dst, bias=bias: e.activation(
                           dst, SC, AF.Identity,
                           bias=bias if bias is not None else 0.0),
                       waits=[etok or tail, cfg["bias_tok"]], inc=True,
                       tag=f"s{u}_ev{h}_{w}")
            ctx.ps_read(0, ev)
            ctx.ps_read(1, ev)
            ev_toks[w] = ev
        tp_toks = []
        pw3 = ctx.ps_wait(3)
        for kc in range(KC):
            tp = p.add("tensor",
                       lambda e, kc=kc: e.transpose(
                           t["PSB"][:, kc * 128:(kc + 1) * 128],
                           vh[:, kc * 128:(kc + 1) * 128], t["identbf"][:, :]),
                       waits=([ev_toks[2], vt_init, ctx.ident_tok] + pw3)
                             if kc == 0 else [],
                       inc=True if kc == KC - 1 else None, tag=f"s{u}_vt{h}")
            tp_toks.append(tp)
        vt_ev = p.add("vector",
                      lambda e: e.tensor_copy(
                          vt.rearrange("p (k c) -> p k c", c=129)[:, :, 0:128],
                          t["PSB"][:, :].rearrange("p (k c) -> p k c", c=128)),
                      waits=[tp_toks[-1]], inc=True, tag=f"s{u}_vte{h}")
        ctx.ps_read(3, vt_ev)

        nqt = 16
        for qt in range(nqt):
            Wk = (qt + 1) * 128 if cfg["causal"] else 2048
            nkc = Wk // 128
            sc_w = ctx.ps_wait(0) + ctx.ps_wait(1)
            sc_tail = None
            nmm = (Wk + 511) // 512
            for i in range(nmm):
                c0, c1 = i * 512, min(Wk, (i + 1) * 512)
                sc_tail = p.add(
                    "tensor",
                    lambda e, c0=c0, c1=c1, qt=qt: e.matmul(
                        SC[:, c0:c1], qh[:, qt * 128:(qt + 1) * 128],
                        kh[:, c0:c1], start=True, stop=True),
                    waits=([ev_toks[0], ev_toks[1]] + sc_w) if i == 0 else [],
                    inc=True if i == nmm - 1 else None, tag=f"s{u}_sc{h}_{qt}")
            nslot = negm[:, qt % 2:qt % 2 + 1]
            mxs = p.add("vector",
                        lambda e, nslot=nslot, Wk=Wk: e.reduce_max(
                            nslot, SC[:, 0:Wk], AX.X, negate=True),
                        waits=[sc_tail] + ctx.buf_wait(("negm", qt % 2)),
                        inc=True, tag=f"s{u}_mx{h}_{qt}")
            ex = p.add("scalar",
                       lambda e, nslot=nslot, Wk=Wk: e.activation(
                           attn[:, 0:Wk], SC[:, 0:Wk], AF.Exp,
                           bias=nslot),
                       waits=[mxs] + ctx.buf_wait(("attn",)), inc=True,
                       tag=f"s{u}_ex{h}_{qt}")
            ctx.buf_read(("negm", qt % 2), ex)
            ctx.ps_read(0, ex)
            ctx.ps_read(1, ex)
            em = ex
            if cfg["causal"]:
                em = p.add("vector",
                           lambda e, qt=qt: e.tensor_tensor(
                               attn[:, qt * 128:(qt + 1) * 128],
                               attn[:, qt * 128:(qt + 1) * 128],
                               t["tribf"][:, :], ALU.mult),
                           waits=[ex, ctx.tri_tok], inc=True,
                           tag=f"s{u}_msk{h}_{qt}")
            pw3 = ctx.ps_wait(3)
            tp_tail = None
            for kc in range(nkc):
                tp_tail = p.add(
                    "tensor",
                    lambda e, kc=kc: e.transpose(
                        t["PSB"][:, kc * 128:(kc + 1) * 128],
                        attn[:, kc * 128:(kc + 1) * 128], t["identbf"][:, :]),
                    waits=([em] + pw3) if kc == 0 else [],
                    inc=True if kc == nkc - 1 else None,
                    tag=f"s{u}_at{h}_{qt}")
            at_ev = p.add("vector",
                          lambda e, Wk=Wk: e.tensor_copy(attT[:, 0:Wk],
                                                         t["PSB"][:, 0:Wk]),
                          waits=[tp_tail], inc=True, tag=f"s{u}_ae{h}_{qt}")
            ctx.ps_read(3, at_ev)
            ctx.buf_read(("attn",), at_ev)
            avp = AVP[qt % 2]
            av_w = ctx.ps_wait(2) if qt < 2 else []
            av_tail = None
            for kc in range(nkc):
                av_tail = p.add(
                    "tensor",
                    lambda e, kc=kc, avp=avp: e.matmul(
                        avp[:, 0:129], attT[:, kc * 128:(kc + 1) * 128],
                        vt[:, kc * 129:(kc + 1) * 129],
                        start=(kc == 0), stop=(kc == nkc - 1)),
                    waits=([at_ev, vt_ev] + av_w +
                           ctx.buf_wait(("avp", qt % 2))) if kc == 0 else [],
                    inc=True if kc == nkc - 1 else None, tag=f"s{u}_av{h}_{qt}")
            if DBG_AV is not None and h == DBG_AV[0] and qt == DBG_AV[1]:
                t["avdump"] = nc.dram_tensor("avdump", [128, 2048], F32,
                                             kind="ExternalOutput")
                t["avdump2"] = nc.dram_tensor("avdump2", [128, 256], F32,
                                              kind="ExternalOutput")
                stg = t["trig"]
                dtok1 = p.add("vector",
                              lambda e, avp=avp: e.tensor_copy(
                                  stg[:, 0:512], avp),
                              waits=[av_tail], inc=True, tag="dbgav1")
                dtok2 = p.add("vector",
                              lambda e, nslot=nslot: e.tensor_copy(
                                  stg[:, 512:513], nslot),
                              waits=[mxs], inc=True, tag="dbgav2")
                dtok3 = p.add("vector",
                              lambda e: e.tensor_copy(
                                  stg[:, 640:768], attn[:, 0:128]),
                              waits=[em], inc=True, tag="dbgav2b")
                dtok4 = p.add("vector",
                              lambda e: e.tensor_copy(
                                  stg[:, 768:896], SC[:, 0:128]),
                              waits=[sc_tail], inc=True, tag="dbgav2c")
                ctx.ps_read(0, dtok4)
                ctx.buf_read(("avp", qt % 2), dtok1)
                ctx.buf_read(("attn",), dtok3)
                dd = p.add("sync",
                           lambda e: e.dma_start(out=t["avdump"].ap(),
                                                 in_=stg[:, :]),
                           waits=[dtok1, dtok2, dtok3, dtok4], inc="dx",
                           tag="dbgav3")
                p.add("sync", lambda e: e.nop(), waits=[dd])
            rs = t["rsn"][:, qt % 4:qt % 4 + 1]
            rc = p.add("vector",
                       lambda e, rs=rs, avp=avp: e.reciprocal(
                           rs, avp[:, 128:129]),
                       waits=[av_tail], tag=f"s{u}_rs{h}_{qt}")
            p.add("vector", lambda e: e.tensor_copy(t["fence"][:, :],
                                                    t["tribf"][:, 0:32]),
                  tag=f"s{u}_fn{h}_{qt}")
            nq = p.add("vector",
                       lambda e, rs=rs, avp=avp: e.tensor_scalar_mul(
                           outq, avp[:, 0:128], rs),
                       waits=ctx.buf_wait(("outq",)), inc=True,
                       tag=f"s{u}_nq{h}_{qt}")
            ctx.buf_read(("avp", qt % 2), nq)
            if DBG_AV is not None and h == DBG_AV[0] and qt == DBG_AV[1]:
                stg = t["trig"]
                d3 = p.add("vector",
                           lambda e, rs=rs: e.tensor_copy(stg[:, 1536:1537],
                                                          rs),
                           waits=[nq], inc=True, tag="dbgav4")
                d4 = p.add("vector",
                           lambda e: e.tensor_copy(stg[:, 1600:1728], outq),
                           inc=True, tag="dbgav5")
                dd2 = p.add("sync",
                            lambda e: e.dma_start(
                                out=t["avdump2"].ap(), in_=stg[:, 1536:1792]),
                            waits=[d4], inc="dx", tag="dbgav6")
                p.add("sync", lambda e: e.nop(), waits=[dd2])
            pw3 = ctx.ps_wait(3)
            ot = p.add("tensor",
                       lambda e: e.transpose(t["PSB"][:, 1920:2048], outq,
                                             t["identbf"][:, :]),
                       waits=[nq] + pw3, inc=True, tag=f"s{u}_ot{h}_{qt}")
            oe = p.add("scalar",
                       lambda e, qt=qt: e.copy(oh[:, qt * 128:(qt + 1) * 128],
                                               t["PSB"][:, 1920:2048]),
                       waits=[ot] + ctx.buf_wait(("oh", qt)), inc=True,
                       tag=f"s{u}_oe{h}_{qt}")
            ctx.ps_read(3, oe)
            ctx.buf_read(("outq",), ot)
            ctx.buf_read(("oh", qt), None)
        ohd = p.add("sync",
                    lambda e, h=h: e.dma_start(
                        out=t["aoc"].ap()[h * 128:(h + 1) * 128, :],
                        in_=oh),
                    waits=[oe], inc="dx", tag=f"s{u}_ohd{h}")
        oh_dma.append(ohd)
        for qt in range(16):
            ctx.buf_read(("oh", qt), ohd)

    ctx.ps_read(2, nq)

    cc2 = p.add("gpsimd",
                lambda e: e.collective_compute(
                    "AllGather", ALU.bypass,
                    replica_groups=[[0, 1], [2, 3], [4, 5], [6, 7]],
                    ins=[t["aoc"].ap().opt()], outs=[t["aog"].ap().opt()]),
                waits=[oh_dma[-1]], inc=CCSEM, tag=f"s{u}_aocc")
    rb2 = []
    for half in range(2):
        arena = t["a1b"] if half == 0 else t["a2"]
        roff = ctx.roff_reg if half == 0 else ctx.roff2_reg
        rb2.append(p.add(
            "sync",
            lambda e, half=half, arena=arena, roff=roff: e.dma_start(
                out=arena.ap()[:, 0:8 * T].rearrange("p (k t) -> p k t", k=8),
                in_=bass.AP(t["aog"], roff,
                            [[SEQ, 128], [128 * SEQ, 8], [1, T]],
                            dep_tracking_offset=half * 8 * 128 * SEQ)),
            waits=[cc2], inc="dx", tag=f"s{u}_aorb{half}"))
    aout_ready = rb2[-1]

    def aout(kc):
        arena = t["a1b"] if kc < 8 else t["a2"]
        return arena[:, (kc % 8) * T:(kc % 8) * T + T]

    pre_o_tok = None
    if cfg.get("pre_o"):
        pre_o_tok = cfg["pre_o"](aout, aout_ready)

    last_nq = None
    for ft in range(KC):
        slot, queue = next_slot()
        wtok, base = cfg["make_wo"](ft, slot, queue, f"s{u}_wo{ft}")
        pr = ft % 2
        ps = ctx.ps(pr)
        mw = [wtok, aout_ready, pre_o_tok] + ctx.ps_wait(pr)
        tail = None
        closes = not cfg["has_extra_o"]
        for kc in range(KC):
            for hh in range(2):
                tail = p.add(
                    "tensor",
                    lambda e, kc=kc, hh=hh, ps=ps, base=base, closes=closes:
                    e.matmul(
                        ps[:, hh * 512:(hh + 1) * 512],
                        t["wbf"][:, base + kc * 128:base + (kc + 1) * 128],
                        aout(kc)[:, hh * 512:(hh + 1) * 512],
                        start=(kc == 0), stop=(closes and kc == KC - 1)),
                    waits=mw if kc == 0 and hh == 0 else [],
                    inc=True if closes and kc == KC - 1 and hh == 1 else None,
                    tag=f"s{u}_omm{ft}")
        etok = cfg["extra_mm_o"](ft, ps, f"s{u}_ox{ft}")
        ctx.buf_read(("wbf", ctx._cur_wslot), etok or tail)
        bias = cfg["bias_o"](ft)
        rft = t["R"][:, ft * T:(ft + 1) * T]
        r_tok = p.add("vector",
                      lambda e, ps=ps, rft=rft, bias=bias:
                      e.scalar_tensor_tensor(
                          rft, ps, bias if bias is not None else 0.0, rft,
                          ALU.add, ALU.add),
                      waits=[etok or tail], inc=True, tag=f"s{u}_racc{ft}")
        ctx.ps_read(pr, r_tok)
    ctx.buf_read("lnb", r_tok)
    ctx.resid_ready = r_tok
    return r_tok


def emit_attn_a(ctx):
    nc, p, t = ctx.nc, ctx.p, ctx.t
    u = ctx.nuid()
    lnb, a1b, a2, sq = t["lnb"], t["a1b"], t["a2"], t["sq"]
    trig, rsc = t["trig"], t["rsc"]
    norm_done = emit_ln(ctx)

    win_d = t["w_a0_in"].ap()
    wout_d = t["w_a0_out"].ap()
    trig_d = t["trig_a0"].ap()

    sqA = sq[:, 0:1024]
    sqB = sq[:, 1024:2048]
    cosT = trig[:, 0:1024]
    sinT = trig[:, 1024:2048]

    btok = p.add("sync", lambda e: e.dma_start(out=t["bias"][:, :],
                                               in_=t["b_a0"].ap()),
                 waits=ctx.buf_wait("bias"), inc="dx", tag=f"a{u}_bias")
    stok = p.add("sync", lambda e: e.dma_start(out=t["shufst"][:, :],
                                               in_=t["w_a0_shuf"].ap()),
                 waits=ctx.buf_wait("shufst"), inc="dx", tag=f"a{u}_shuf")
    sctok = p.add("gpsimd", lambda e: e.tensor_copy(t["shufbf"][:, :],
                                                    t["shufst"][:, :]),
                  waits=[stok], inc=True, tag=f"a{u}_shufc")
    ctx.buf_read("shufst", sctok)

    rot = [0]

    def proj_group(slab_ap, rhs_buf, extra_waits, tag):
        slot = rot[0] % 2
        queue = "sync" if slot == 0 else "scalar"
        ctok, base = stream_weight(ctx, slab_ap, slot, queue, tag)
        pr = rot[0] % 2
        rot[0] += 1
        ps = ctx.ps(pr)
        mw = [ctok] + extra_waits + ctx.ps_wait(pr)
        tail = None
        for kc in range(KC):
            for h in range(2):
                tail = p.add(
                    "tensor",
                    lambda e, kc=kc, h=h, ps=ps, base=base, rhs_buf=rhs_buf:
                    e.matmul(ps[:, h * 512:(h + 1) * 512],
                             t["wbf"][:, base + kc * 128:base + (kc + 1) * 128],
                             rhs_buf[:, kc * T + h * 512:kc * T + (h + 1) * 512],
                             start=(kc == 0), stop=(kc == KC - 1)),
                    waits=mw if kc == 0 and h == 0 else [],
                    inc=True if kc == KC - 1 and h == 1 else None,
                    tag=f"{tag}_mm")
        ctx.buf_read(("wbf", slot), tail)
        return pr, tail

    def rope(c, src, dst, evac_tok, pr, tag):
        ps = ctx.ps(pr)
        trig_w = ctx.buf_wait("trig")
        dtok = p.add("sync",
                     lambda e, c=c: e.dma_start(
                         out=t["trig"].ap().rearrange("p (k t) -> p k t", k=2),
                         in_=trig_d[:, c].rearrange("k p t -> p k t")),
                     waits=trig_w, inc="dwa", tag=f"{tag}_trig")
        sh_tail = None
        for h in range(2):
            sh_tail = p.add("tensor",
                            lambda e, h=h, ps=ps, src=src: e.matmul(
                                ps[:, h * 512:(h + 1) * 512], t["shufbf"][:, :],
                                src[:, h * 512:(h + 1) * 512],
                                start=True, stop=True),
                            waits=[evac_tok, sctok] if h == 0 else [],
                            inc=True if h == 1 else None, tag=f"{tag}_shuf")
        u1 = p.add("vector",
                   lambda e, ps=ps: e.tensor_tensor(rsc[:, :], ps, sinT,
                                                    ALU.mult),
                   waits=[sh_tail, dtok], inc=True, tag=f"{tag}_u1")
        ctx.ps_read(pr, u1)
        p.add("vector",
              lambda e, src=src, dst=dst: e.tensor_tensor(dst, src, cosT,
                                                          ALU.mult),
              tag=f"{tag}_u2")
        rtok = p.add("vector",
                     lambda e, dst=dst: e.tensor_tensor(dst, dst, rsc[:, :],
                                                        ALU.add),
                     inc=True, tag=f"{tag}_u3")
        ctx.buf_read("trig", rtok)
        return rtok

    s_k2 = t["PS"][0:1, 2048:3072]
    RNb = ctx.ps(2)
    bias = t["bias"]

    k2_tail = None
    w2 = ctx.ps_wait(2)
    for c in range(KC):
        ft = 16 + c
        dst = a2[:, c * T:(c + 1) * T]
        pr, mm = proj_group(win_d[ft], lnb,
                            [norm_done, btok] if c == 0 else [], f"a{u}_k{c}")
        if c < 8:
            ev = p.add("scalar",
                       lambda e, pr=pr, ft=ft: e.activation(
                           sqA, ctx.ps(pr), AF.Identity,
                           bias=bias[:, ft:ft + 1]),
                       waits=[mm] + ctx.buf_wait(("sq", 0)), inc=True,
                       tag=f"a{u}_kev{c}")
            ctx.ps_read(pr, ev)
            rtok = rope(c, sqA, dst, ev, pr, f"a{u}_kr{c}")
            ctx.buf_read(("sq", 0), rtok)
            ready = rtok
        else:
            ready = p.add("scalar",
                          lambda e, pr=pr, ft=ft, dst=dst: e.activation(
                              dst, ctx.ps(pr), AF.Identity,
                              bias=bias[:, ft:ft + 1]),
                          waits=[mm], inc=True, tag=f"a{u}_kev{c}")
            ctx.ps_read(pr, ready)
        sq_tok = p.add("scalar",
                       lambda e, dst=dst: e.activation(sqB, dst, AF.Square),
                       waits=[ready] + ctx.buf_wait(("sq", 1)), inc=True,
                       tag=f"a{u}_ksq{c}")
        ctx.buf_read(("a2sq", c), sq_tok)
        for h in range(2):
            k2_tail = p.add(
                "tensor",
                lambda e, h=h: e.matmul(
                    s_k2[:, h * 512:(h + 1) * 512], t["onec"][:, :],
                    sqB[:, h * 512:(h + 1) * 512],
                    start=(c == 0), stop=(c == KC - 1), skip_group_check=True),
                waits=([sq_tok] + (w2 if c == 0 else [])) if h == 0 else [],
                inc=True if h == 1 else None, tag=f"a{u}_k2_{c}_{h}")
        ctx.buf_read(("sq", 1), k2_tail)

    sr = t["strips"][0:1, 0:1024]
    sq_tok = p.add("scalar", lambda e: e.activation(sr, s_k2, AF.Sqrt),
                   waits=[k2_tail], inc=True, tag=f"a{u}_rn_sqrt")
    p.add("vector", lambda e: e.tensor_scalar_max(sr, sr, 1e-12),
          waits=[sq_tok], tag=f"a{u}_rn_max")
    rcp = p.add("vector", lambda e: e.reciprocal(sr, sr), inc=True,
                tag=f"a{u}_rn_rcp")
    rnb_tok = None
    for h in range(2):
        rnb_tok = p.add("tensor",
                        lambda e, h=h: e.matmul(
                            RNb[:, h * 512:(h + 1) * 512], t["oner"][:, :],
                            sr[:, h * 512:(h + 1) * 512],
                            start=True, stop=True),
                        waits=[rcp, sq_tok] if h == 0 else [],
                        inc=True if h == 1 else None, tag=f"a{u}_rnb{h}")

    ttr_tok = None
    for c in range(KC):
        ft = 32 + c
        kv = a2[:, c * T:(c + 1) * T]
        pr, mm = proj_group(win_d[ft], lnb, [], f"a{u}_v{c}")
        ev = p.add("scalar",
                   lambda e, pr=pr, ft=ft: e.activation(
                       sqA, ctx.ps(pr), AF.Identity, bias=bias[:, ft:ft + 1]),
                   waits=[mm] + ctx.buf_wait(("sq", 0)), inc=True,
                   tag=f"a{u}_vev{c}")
        ctx.ps_read(pr, ev)
        kv_tok = p.add("vector",
                       lambda e, kv=kv: e.tensor_tensor(kv, kv, sqA, ALU.mult),
                       waits=[ev] + ctx.buf_wait(("a2sq", c)), inc=True,
                       tag=f"a{u}_kv{c}")
        ctx.buf_read(("sq", 0), kv_tok)
        p.add("vector",
              lambda e, kv=kv: e.tensor_tensor(sqB, kv, RNb, ALU.mult),
              waits=[rnb_tok, kv_tok] + ctx.buf_wait(("sq", 1)),
              tag=f"a{u}_ttp{c}")
        ttr_tok = p.add("vector",
                        lambda e, c=c: e.reduce_sum(
                            t["amine"][:, c:c + 1], sqB, AX.X),
                        inc=True, tag=f"a{u}_ttr{c}")
        ctx.buf_read(("sq", 1), ttr_tok)
    ctx.ps_read(2, ttr_tok)

    ttr_tok = p.add("vector",
                    lambda e: e.tensor_copy(sqB[:, 0:64], sqB[:, 64:128]),
                    inc=True, tag=f"a{u}_fence")
    d1 = p.add("sync", lambda e: e.dma_start(out=t["cc_a_in"].ap(),
                                             in_=t["amine"][:, :]),
               waits=[ttr_tok], inc="dx", tag=f"a{u}_adma")
    cc = p.add("gpsimd",
               lambda e: e.collective_compute(
                   "AllReduce", ALU.add,
                   replica_groups=[[0, 1], [2, 3], [4, 5], [6, 7]],
                   ins=[t["cc_a_in"].ap().opt()],
                   outs=[t["cc_a_out"].ap().opt()]),
               waits=[d1], inc=CCSEM, tag=f"a{u}_cc")

    rope_toks = []
    for c in range(KC):
        dst = a1b[:, c * T:(c + 1) * T]
        pr, mm = proj_group(win_d[c], lnb, [], f"a{u}_q{c}")
        if c < 8:
            ev = p.add("scalar",
                       lambda e, pr=pr, c=c: e.activation(
                           sqA, ctx.ps(pr), AF.Identity, bias=bias[:, c:c + 1]),
                       waits=[mm] + ctx.buf_wait(("sq", 0)), inc=True,
                       tag=f"a{u}_qev{c}")
            ctx.ps_read(pr, ev)
            rtok = rope(c, sqA, dst, ev, pr, f"a{u}_qr{c}")
            ctx.buf_read(("sq", 0), rtok)
            rope_toks.append(rtok)
        else:
            ev = p.add("scalar",
                       lambda e, pr=pr, c=c, dst=dst: e.activation(
                           dst, ctx.ps(pr), AF.Identity, bias=bias[:, c:c + 1]),
                       waits=[mm], inc=True, tag=f"a{u}_qev{c}")
            ctx.ps_read(pr, ev)
            rope_toks.append(ev)

    d2 = p.add("sync", lambda e: e.dma_start(out=t["afull"][:, :],
                                             in_=t["cc_a_out"].ap()),
               waits=[cc], inc="dx", tag=f"a{u}_ardma")
    y_toks = []
    for c in range(KC):
        y_toks.append(p.add(
            "scalar",
            lambda e, c=c: e.activation(a1b[:, c * T:(c + 1) * T],
                                        a1b[:, c * T:(c + 1) * T],
                                        AF.Identity, scale=t["afull"][:, c:c + 1]),
            waits=[d2, rope_toks[c]], inc=True, tag=f"a{u}_y{c}"))

    r_tok = None
    for ft in range(KC):
        slot = rot[0] % 2
        queue = "sync" if slot == 0 else "scalar"
        ctok, base = stream_weight(ctx, wout_d[ft], slot, queue, f"a{u}_o{ft}")
        pr = rot[0] % 2
        rot[0] += 1
        ps = ctx.ps(pr)
        mw = [ctok, y_toks[-1]] + ctx.ps_wait(pr)
        tail = None
        for kc in range(KC):
            for h in range(2):
                tail = p.add(
                    "tensor",
                    lambda e, kc=kc, h=h, ps=ps, base=base: e.matmul(
                        ps[:, h * 512:(h + 1) * 512],
                        t["wbf"][:, base + kc * 128:base + (kc + 1) * 128],
                        a1b[:, kc * T + h * 512:kc * T + (h + 1) * 512],
                        start=(kc == 0), stop=(kc == KC - 1)),
                    waits=mw if kc == 0 and h == 0 else [],
                    inc=True if kc == KC - 1 and h == 1 else None,
                    tag=f"a{u}_omm{ft}")
        ctx.buf_read(("wbf", slot), tail)
        rft = t["R"][:, ft * T:(ft + 1) * T]
        r1 = p.add("vector",
                   lambda e, ps=ps, ft=ft, rft=rft: e.scalar_tensor_tensor(
                       rft, ps, bias[:, 48 + ft:48 + ft + 1], rft,
                       ALU.add, ALU.add),
                   waits=[tail], inc=True, tag=f"a{u}_racc{ft}")
        ctx.ps_read(pr, r1)
        r_tok = p.add("vector",
                      lambda e, ft=ft, rft=rft: e.tensor_tensor(
                          rft, rft, lnb[:, ft * T:(ft + 1) * T], ALU.add),
                      inc=True, tag=f"a{u}_rln{ft}")
    ctx.buf_read("lnb", r_tok)
    ctx.buf_read("bias", r_tok)
    ctx.resid_ready = r_tok
    return r_tok


BLOCK_EMITTERS = {}


def build_program(blocks):
    nc = bass.Bass("TRN2", target_bir_lowering=False,
                   dynamic_dma_scratch_size=2048)
    plan = Planner()
    ctx = Ctx(nc, plan)
    names = declare_ios(ctx, blocks)
    has_attn = any(b in blocks for b in ATTN_BLOCKS)
    with ExitStack() as es:
        alloc_onchip(ctx, es)
        if has_attn:
            ctx.roff_reg = es.enter_context(nc.sync.register("roff"))
            ctx.roff2_reg = es.enter_context(nc.sync.register("roff2"))
        sems = {}
        for s in plan.semv:
            sems[s] = es.enter_context(nc.semaphore(s))
        blk = es.enter_context(nc.Block())

        emit_setup(ctx)
        if has_attn:
            emit_attn_consts(ctx)
        for b in blocks:
            BLOCK_EMITTERS[b](ctx)
        emit_store(ctx)

        for eng in ENGS:
            def body(h, eng=eng):
                for fn, waits, (sem, amt), tag in plan.ops[eng]:
                    for s, v in waits:
                        h.wait_ge(sems[s], v)
                    ins = fn(h)
                    if sem is not None:
                        ins.then_inc(sems[sem], amt)
            getattr(blk, eng)(body)
    return nc, names


def emit_attn_a1(ctx):
    p, t = ctx.p, ctx.t
    bias_tok = p.add("sync",
                     lambda e: e.dma_start(out=t["bias"][:, :],
                                           in_=t["b_a4"].ap()),
                     waits=ctx.buf_wait("bias"), inc="dx", tag="a4_bias")
    sc = 1.0 / math.sqrt(128.0)
    bias_tok = p.add("scalar",
                     lambda e: e.activation(t["bias"][:, 0:8],
                                            t["bias"][:, 0:8],
                                            AF.Identity, scale=sc),
                     waits=[bias_tok], inc=True, tag="a4_biassc")
    wq = t["w_a4_qkv"].ap()
    wo = t["w_a4_o"].ap()
    cfg = dict(
        name="a4", causal=True, has_extra_w=False, has_extra_o=False,
        scale=sc,
        bias_tok=bias_tok,
        make_w=lambda h, w, slot, queue, tag:
            stream_weight(ctx, wq[h, w], slot, queue, tag,
                          scale=sc if w == 0 else None),
        extra_mm=lambda h, w, ps, tag: None,
        bias_col=lambda w, h: t["bias"][:, w * 8 + h:w * 8 + h + 1],
        make_wo=lambda ft, slot, queue, tag:
            stream_weight(ctx, wo[ft], slot, queue, tag),
        extra_mm_o=lambda ft, ps, tag: None,
        bias_o=lambda ft: t["bias"][:, 48 + ft:48 + ft + 1],
    )
    rt = emit_softmax_attn(ctx, cfg)
    ctx.buf_read("bias", rt)
    return rt


def emit_ffn1(ctx):
    nc, p, t = ctx.nc, ctx.p, ctx.t
    u = ctx.nuid()
    lnb, a1b, a2, sq, suf = t["lnb"], t["a1b"], t["a2"], t["sq"], t["suf"]
    norm_done = emit_ln(ctx)

    sd = p.add("sync", lambda e: e.dma_start(out=suf[:, :],
                                             in_=t["v_f1"].ap()),
               waits=ctx.buf_wait("suf"), inc="dx", tag=f"f1{u}_suf")
    s1_ = p.add("scalar",
                lambda e: e.activation(suf[:, 80:208], suf[:, 80:208],
                                       AF.Identity,
                                       scale=1.0 / math.sqrt(2048.0)),
                waits=[sd], inc=True, tag=f"f1{u}_sc1")
    suf_tok = p.add("scalar",
                    lambda e: e.activation(suf[:, 208:224], suf[:, 208:224],
                                           AF.Identity,
                                           scale=1.0 / math.sqrt(8192.0)),
                    inc=True, tag=f"f1{u}_sc2")
    hd_ = p.add("sync", lambda e: e.dma_start(out=t["shufst"][:, :],
                                              in_=t["h128"].ap()),
                waits=ctx.buf_wait("shufst"), inc="dx", tag=f"f1{u}_h128")
    h128_tok = p.add("gpsimd",
                     lambda e: e.tensor_copy(t["h128bf"][:, :],
                                             t["shufst"][:, :]),
                     waits=[hd_], inc=True, tag=f"f1{u}_h128c")
    ctx.buf_read("shufst", h128_tok)

    su_tok = None
    for c in range(KC):
        su_tok = p.add("gpsimd",
                       lambda e, c=c: e.tensor_scalar_mul(
                           lnb[:, c * T:(c + 1) * T],
                           lnb[:, c * T:(c + 1) * T], suf[:, c:c + 1]),
                       waits=[norm_done, suf_tok] if c == 0 else [],
                       inc=True if c == KC - 1 else None, tag=f"f1{u}_su{c}")

    rotc = [0]

    def rot_pair():
        pr = rotc[0] % 3
        rotc[0] += 1
        return pr

    ev1 = {}
    for c in range(KC):
        pr = rot_pair()
        ps = ctx.ps(pr)
        mw = [su_tok, h128_tok] + ctx.ps_wait(pr)
        tail = None
        for hh in range(2):
            tail = p.add("tensor",
                         lambda e, c=c, hh=hh, ps=ps: e.matmul(
                             ps[:, hh * 512:(hh + 1) * 512],
                             t["h128bf"][:, :],
                             lnb[:, c * T + hh * 512:c * T + (hh + 1) * 512],
                             start=True, stop=True),
                         waits=mw if hh == 0 else [],
                         inc=True if hh == 1 else None, tag=f"f1{u}_w1{c}")
        ev1[c] = p.add("scalar",
                       lambda e, c=c, ps=ps: e.activation(
                           a1b[:, c * T:(c + 1) * T], ps, AF.Identity),
                       waits=[tail], inc=True, tag=f"f1{u}_e1{c}")
        ctx.ps_read(pr, ev1[c])

    bufs = [a1b, lnb]
    toks = dict(ev1)
    for st, s in enumerate([1, 2, 4, 8]):
        src = bufs[st % 2]
        dst = bufs[(st + 1) % 2]
        ntoks = {}
        for j in range(0, KC, 2 * s):
            for k in range(s):
                a, b = j + k, j + k + s
                wa = [toks[a], toks[b]]
                t1 = p.add("vector",
                           lambda e, a=a, b=b, src=src, dst=dst:
                           e.tensor_tensor(dst[:, a * T:(a + 1) * T],
                                           src[:, a * T:(a + 1) * T],
                                           src[:, b * T:(b + 1) * T],
                                           ALU.add),
                           waits=wa, tag=f"f1{u}_b{st}a{a}")
                t2 = p.add("vector",
                           lambda e, a=a, b=b, src=src, dst=dst:
                           e.tensor_tensor(dst[:, b * T:(b + 1) * T],
                                           src[:, a * T:(a + 1) * T],
                                           src[:, b * T:(b + 1) * T],
                                           ALU.subtract),
                           inc=True, tag=f"f1{u}_b{st}b{b}")
                ntoks[a] = t2
                ntoks[b] = t2
        toks = ntoks
    x1_tok = toks[15]

    def g_ap(hc, c0, c1):
        base = a2 if hc < 32 else lnb
        off = (hc % 32) * 512
        return base[:, off + c0:off + c1]

    wg_d = t["w_f1_gate"].ap()
    wu_d = t["w_f1_up"].ap()
    wd_d = t["w_f1_down"].ap()
    wbf = t["wbf"]
    r_tok = None

    for th in range(2):
        tc = th * 512
        g_toks = []
        for hc in range(HC):
            cg, bg = stream_weight(ctx, wg_d[hc], 0, "sync", f"f1{u}g{th}_{hc}")
            cu, bu = stream_weight(ctx, wu_d[hc], 1, "scalar",
                                   f"f1{u}u{th}_{hc}")
            pg = rot_pair()
            pu = rot_pair()
            gate_ps = ctx.ps(pg, 0, 512)
            up_ps = ctx.ps(pu, 0, 512)
            mwg = [cg, x1_tok] + ctx.ps_wait(pg)
            mwu = [cu] + ctx.ps_wait(pu)
            gt = ut = None
            for kc in range(KC):
                gt = p.add("tensor",
                           lambda e, kc=kc, bg=bg, gate_ps=gate_ps, tc=tc:
                           e.matmul(gate_ps,
                                    wbf[:, bg + kc * 128:bg + (kc + 1) * 128],
                                    a1b[:, kc * T + tc:kc * T + tc + 512],
                                    start=(kc == 0), stop=(kc == KC - 1)),
                           waits=mwg if kc == 0 else [],
                           inc=True if kc == KC - 1 else None,
                           tag=f"f1{u}_gmm{th}_{hc}")
            for kc in range(KC):
                ut = p.add("tensor",
                           lambda e, kc=kc, bu=bu, up_ps=up_ps, tc=tc:
                           e.matmul(up_ps,
                                    wbf[:, bu + kc * 128:bu + (kc + 1) * 128],
                                    a1b[:, kc * T + tc:kc * T + tc + 512],
                                    start=(kc == 0), stop=(kc == KC - 1)),
                           waits=mwu if kc == 0 else [],
                           inc=True if kc == KC - 1 else None,
                           tag=f"f1{u}_umm{th}_{hc}")
            ctx.buf_read(("wbf", 0), gt)
            ctx.buf_read(("wbf", 1), ut)
            sg = sq[:, 0:512]
            ub = sq[:, 1024:1536]
            silu_tok = p.add("scalar",
                            lambda e, sg=sg, gate_ps=gate_ps, hc=hc:
                            e.activation(sg, gate_ps, AF.Silu,
                                         scale=suf[:, 144 + hc:145 + hc]),
                            waits=[gt] + ctx.buf_wait(("sq", 0)), inc=True,
                            tag=f"f1{u}_si{th}_{hc}")
            ctx.ps_read(pg, silu_tok)
            up_tok = p.add("scalar",
                           lambda e, ub=ub, up_ps=up_ps, hc=hc:
                           e.activation(ub, up_ps, AF.Identity,
                                        scale=suf[:, 80 + hc:81 + hc]),
                           waits=[ut] + ctx.buf_wait(("sq", 2)), inc=True,
                           tag=f"f1{u}_ue{th}_{hc}")
            ctx.ps_read(pu, up_tok)
            g_tok = p.add("vector",
                          lambda e, hc=hc, sg=sg, ub=ub:
                          e.scalar_tensor_tensor(
                              g_ap(hc, 0, 512), sg, suf[:, 16 + hc:17 + hc],
                              ub, ALU.mult, ALU.mult),
                          waits=[silu_tok, up_tok] +
                                ctx.buf_wait(("g1", hc)),
                          inc=True, tag=f"f1{u}_g{th}_{hc}")
            ctx.buf_read(("sq", 0), g_tok)
            ctx.buf_read(("sq", 2), g_tok)
            g_toks.append(g_tok)

        z_toks = {}
        for hc in range(HC):
            pr = rot_pair()
            ps = ctx.ps(pr, 0, 512)
            mm = p.add("tensor",
                       lambda e, hc=hc, ps=ps: e.matmul(
                           ps, t["h128bf"][:, :], g_ap(hc, 0, 512),
                           start=True, stop=True),
                       waits=[g_toks[hc]] + ctx.ps_wait(pr),
                       inc=True, tag=f"f1{u}_z{th}_{hc}")
            z_toks[hc] = p.add("scalar",
                               lambda e, hc=hc, ps=ps: e.activation(
                                   g_ap(hc, 0, 512), ps, AF.Identity),
                               waits=[mm], inc=True, tag=f"f1{u}_ze{th}_{hc}")
            ctx.ps_read(pr, z_toks[hc])
        toks2 = dict(z_toks)
        for st, s in enumerate([1, 2, 4, 8, 16, 32]):
            ntoks = {}
            pi = 0
            for j in range(0, HC, 2 * s):
                for k in range(s):
                    a, b = j + k, j + k + s
                    eng = "vector" if pi % 2 == 0 else "gpsimd"
                    tmp = sq[:, 0:512] if eng == "vector" else sq[:, 512:1024]
                    wa = [toks2[a], toks2[b]]
                    if st > 0:
                        wa += [pv for pv in stage_bar]
                    p.add(eng,
                          lambda e, a=a, b=b, tmp=tmp: e.tensor_tensor(
                              tmp, g_ap(a, 0, 512), g_ap(b, 0, 512),
                              ALU.subtract),
                          waits=wa, tag=f"f1{u}_c{st}s{a}")
                    p.add(eng,
                          lambda e, a=a, b=b: e.tensor_tensor(
                              g_ap(a, 0, 512), g_ap(a, 0, 512),
                              g_ap(b, 0, 512), ALU.add),
                          tag=f"f1{u}_c{st}a{a}")
                    t2 = p.add(eng,
                               lambda e, b=b, tmp=tmp: e.tensor_copy(
                                   g_ap(b, 0, 512), tmp),
                               inc=True, tag=f"f1{u}_c{st}b{b}")
                    ntoks[a] = t2
                    ntoks[b] = t2
                    pi += 1
            stage_bar = [v for v in
                         {tk[0]: tk for tk in ntoks.values()}.values()]
            toks2 = ntoks
        g_done = stage_bar

        for ft in range(KC):
            pr = rot_pair()
            dn_ps = ctx.ps(pr, 0, 512)
            mm_tail = None
            mw0 = ctx.ps_wait(pr) + g_done
            for sl in range(4):
                cd, bd = stream_weight(ctx, wd_d[ft, sl],
                                       sl % 2, "sync" if sl % 2 == 0
                                       else "scalar",
                                       f"f1{u}d{th}_{ft}_{sl}")
                for i in range(16):
                    gi = sl * 16 + i
                    mm_tail = p.add(
                        "tensor",
                        lambda e, i=i, gi=gi, bd=bd, dn_ps=dn_ps, sl=sl:
                        e.matmul(dn_ps,
                                 wbf[:, bd + i * 128:bd + (i + 1) * 128],
                                 g_ap(gi, 0, 512),
                                 start=(sl == 0 and i == 0),
                                 stop=(sl == 3 and i == 15)),
                        waits=([cd] + (mw0 if sl == 0 else []))
                              if i == 0 else [],
                        inc=True if sl == 3 and i == 15 else None,
                        tag=f"f1{u}_dmm{th}_{ft}")
                ctx.buf_read(("wbf", sl % 2), mm_tail)
            rft = t["R"][:, ft * T + tc:ft * T + tc + 512]
            r_tok = p.add("vector",
                          lambda e, ft=ft, dn_ps=dn_ps, rft=rft:
                          e.scalar_tensor_tensor(
                              rft, dn_ps, suf[:, 208 + ft:209 + ft], rft,
                              ALU.mult, ALU.add),
                          waits=[mm_tail], inc=True,
                          tag=f"f1{u}_racc{th}_{ft}")
            ctx.ps_read(pr, r_tok)
        if th == 0:
            for hc in range(HC):
                ctx.buf_read(("g1", hc), r_tok)
    ctx.buf_read("lnb", r_tok)
    ctx.buf_read("suf", r_tok)
    ctx.resid_ready = r_tok
    return r_tok


def emit_attn_b0(ctx):
    nc, p, t = ctx.nc, ctx.p, ctx.t
    sc = 1.0 / math.sqrt(128.0)
    bias_tok = p.add("sync",
                     lambda e: e.dma_start(out=t["bias"][:, :],
                                           in_=t["b_a2"].ap()),
                     waits=ctx.buf_wait("bias"), inc="dx", tag="a2_bias")
    bias_tok = p.add("scalar",
                     lambda e: e.activation(t["bias"][:, 0:8],
                                            t["bias"][:, 0:8],
                                            AF.Identity, scale=sc),
                     waits=[bias_tok], inc=True, tag="a2_biassc")
    bd = p.add("sync",
               lambda e: e.dma_start(out=t["wst"][:, 0:1024],
                                     in_=t["w_a2_B"].ap()),
               waits=ctx.buf_wait(("wst", 0)), inc="dwa", tag="a2_bt")
    bc = p.add("gpsimd",
               lambda e: e.tensor_copy(t["blor"][:, :], t["wst"][:, 0:1024]),
               waits=[bd] + ctx.buf_wait("blor"), inc=True, tag="a2_btc")
    ctx.buf_read(("wst", 0), bc)
    wq = t["w_a2_qkv"].ap()
    wo = t["w_a2_o"].ap()
    ul = t["ul"]
    SCp = t["PS"]

    def pre_heads(lnf, lnf_ready):
        ev = None
        for w in range(3):
            up = SCp[0:16, 0:2048]
            mw = [bc, lnf_ready] + ctx.ps_wait(0) + ctx.ps_wait(1)
            tail = None
            for kc in range(KC):
                for q4 in range(4):
                    tail = p.add(
                        "tensor",
                        lambda e, kc=kc, q4=q4, w=w: e.matmul(
                            up[:, q4 * 512:(q4 + 1) * 512],
                            t["blor"][:, (w * KC + kc) * 16:
                                      (w * KC + kc) * 16 + 16],
                            lnf(kc)[:, q4 * 512:(q4 + 1) * 512],
                            start=(kc == 0), stop=(kc == KC - 1)),
                        waits=mw if kc == 0 and q4 == 0 else [],
                        inc=True if kc == KC - 1 and q4 == 3 else None,
                        tag=f"a2_u{w}")
            s = 0.5 * sc if w == 0 else 0.5
            ev = p.add("scalar",
                       lambda e, w=w, s=s: e.activation(
                           ul[w * 32:w * 32 + 16, :], up, AF.Identity,
                           scale=s),
                       waits=[tail] + (ctx.buf_wait("ul") if w == 0 else []),
                       inc=True, tag=f"a2_ue{w}")
            ctx.ps_read(0, ev)
            ctx.ps_read(1, ev)
        return ev

    def extra_mm(h, w, ps, tag):
        tok = None
        base = ctx._cur_wbase
        for q4 in range(4):
            tok = p.add("tensor",
                        lambda e, q4=q4, w=w, ps=ps, base=base: e.matmul(
                            ps[:, q4 * 512:(q4 + 1) * 512],
                            t["wbf"][w * 32:w * 32 + 16,
                                     base + 2048:base + 2176],
                            ul[w * 32:w * 32 + 16,
                               q4 * 512:(q4 + 1) * 512],
                            start=False, stop=True),
                        inc=True if q4 == 3 else None, tag=tag)
        return tok

    def make_w(h, w, slot, queue, tag):
        tok, base = stream_weight(ctx, wq[h, w], slot, queue, tag,
                                  scale=sc if w == 0 else None, width=2176)
        ctx._cur_wbase = base
        return tok, base

    def pre_o(aout, aout_ready):
        up = SCp[0:16, 0:1024]
        mw = [aout_ready] + ctx.ps_wait(0) + ctx.buf_wait("ul")
        tail = None
        for kc in range(KC):
            for hh in range(2):
                tail = p.add(
                    "tensor",
                    lambda e, kc=kc, hh=hh: e.matmul(
                        up[:, hh * 512:(hh + 1) * 512],
                        t["blor"][:, 768 + kc * 16:768 + kc * 16 + 16],
                        aout(kc)[:, hh * 512:(hh + 1) * 512],
                        start=(kc == 0), stop=(kc == KC - 1)),
                    waits=mw if kc == 0 and hh == 0 else [],
                    inc=True if kc == KC - 1 and hh == 1 else None,
                    tag="a2_uo")
        ev = p.add("scalar",
                   lambda e: e.activation(ul[0:16, 0:1024], up, AF.Identity,
                                          scale=0.5),
                   waits=[tail], inc=True, tag="a2_uoe")
        ctx.ps_read(0, ev)
        return ev

    def extra_mm_o(ft, ps, tag):
        tok = None
        base = ctx._cur_wbase
        for hh in range(2):
            tok = p.add("tensor",
                        lambda e, hh=hh, ps=ps, base=base: e.matmul(
                            ps[:, hh * 512:(hh + 1) * 512],
                            t["wbf"][0:16, base + 2048:base + 2176],
                            ul[0:16, hh * 512:(hh + 1) * 512],
                            start=False, stop=True),
                        inc=True if hh == 1 else None, tag=tag)
        return tok

    def make_wo(ft, slot, queue, tag):
        tok, base = stream_weight(ctx, wo[ft], slot, queue, tag, width=2176)
        ctx._cur_wbase = base
        return tok, base

    cfg = dict(
        name="a2", causal=False, has_extra_w=True, has_extra_o=True,
        scale=sc, bias_tok=bias_tok,
        make_w=make_w, extra_mm=extra_mm,
        bias_col=lambda w, h: t["bias"][:, w * 8 + h:w * 8 + h + 1],
        make_wo=make_wo, extra_mm_o=extra_mm_o,
        bias_o=lambda ft: t["bias"][:, 48 + ft:48 + ft + 1],
        pre_heads=pre_heads, pre_o=pre_o,
    )
    rt = emit_softmax_attn(ctx, cfg)
    ctx.buf_read("bias", rt)
    ctx.buf_read("blor", rt)
    ctx.buf_read("ul", rt)
    return rt


def emit_attn_b1(ctx):
    nc, p, t = ctx.nc, ctx.p, ctx.t
    sc = 1.0 / math.sqrt(128.0)
    bias_tok = p.add("sync",
                     lambda e: e.dma_start(out=t["bias"][:, :],
                                           in_=t["b_a6"].ap()),
                     waits=ctx.buf_wait("bias"), inc="dx", tag="a6_bias")
    bias_tok = p.add("scalar",
                     lambda e: e.activation(t["bias"][:, 0:8],
                                            t["bias"][:, 0:8],
                                            AF.Identity, scale=sc),
                     waits=[bias_tok], inc=True, tag="a6_biassc")
    qc = t["w_a6_qkv_c"].ap()
    qs = t["w_a6_qkv_s"].ap()
    oc = t["w_a6_o_c"].ap()
    os_ = t["w_a6_o_s"].ap()
    wbf = t["wbf"]

    def dequant(ca, cb, scaled, tag):
        if scaled:
            fn = lambda e: e.scalar_tensor_tensor(
                wbf[:, 0:2048], wbf[:, 0:2048], sc, wbf[:, SLOTW:SLOTW + 2048],
                ALU.mult, ALU.mult)
        else:
            fn = lambda e: e.tensor_tensor(
                wbf[:, 0:2048], wbf[:, 0:2048], wbf[:, SLOTW:SLOTW + 2048],
                ALU.mult)
        dq = p.add("vector", fn, waits=[ca, cb], inc=True, tag=tag)
        ctx.buf_read(("wbf", 1), dq)
        ctx._cur_wslot = 0
        return dq

    def make_w(h, w, slot, queue, tag):
        ca, _ = stream_weight(ctx, qc[h, w], 0, "sync", tag + "c")
        cb, _ = stream_weight(ctx, qs[h, w], 1, "scalar", tag + "s")
        return dequant(ca, cb, w == 0, tag + "dq"), 0

    def make_wo(ft, slot, queue, tag):
        ca, _ = stream_weight(ctx, oc[ft], 0, "sync", tag + "c")
        cb, _ = stream_weight(ctx, os_[ft], 1, "scalar", tag + "s")
        return dequant(ca, cb, False, tag + "dq"), 0

    cfg = dict(
        name="a6", causal=False, has_extra_w=False, has_extra_o=False,
        scale=sc, bias_tok=bias_tok,
        make_w=make_w, extra_mm=lambda h, w, ps, tag: None,
        bias_col=lambda w, h: t["bias"][:, w * 8 + h:w * 8 + h + 1],
        make_wo=make_wo, extra_mm_o=lambda ft, ps, tag: None,
        bias_o=lambda ft: t["bias"][:, 48 + ft:48 + ft + 1],
    )
    rt = emit_softmax_attn(ctx, cfg)
    ctx.buf_read("bias", rt)
    return rt


BLOCK_EMITTERS[0] = emit_attn_a
BLOCK_EMITTERS[1] = emit_ffn0
BLOCK_EMITTERS[2] = emit_attn_b0
BLOCK_EMITTERS[3] = emit_ffn0
BLOCK_EMITTERS[4] = emit_attn_a1
BLOCK_EMITTERS[5] = emit_ffn1
BLOCK_EMITTERS[6] = emit_attn_b1
BLOCK_EMITTERS[7] = emit_ffn1



def _run_launches(x_fm_per_core, pre, launch_plan):
    state = x_fm_per_core
    for blocks in launch_plan:
        nc, names = build_program(blocks)
        in_maps = []
        for c in range(NCORES):
            m = {"rin": state[c]}
            for n in names:
                m[n] = pre[n][c]
            in_maps.append(m)
        res = run_bass_kernel_spmd(nc, in_maps, core_ids=list(range(NCORES)))
        state = [np.asarray(r["rout"]) for r in res.results]
    return state


def _to_fm(x):
    out = []
    for c in range(NCORES):
        sl = x[c * T:(c + 1) * T]
        out.append(np.ascontiguousarray(
            sl.T.reshape(KC, 128, T).astype(np.float32)))
    return out


def _from_fm(state):
    x = np.empty((NCORES * T, D), np.float32)
    for c in range(NCORES):
        x[c * T:(c + 1) * T] = state[c].reshape(D, T).T
    return x


def kernel(**inp):
    x = np.asarray(inp["L0_attn_a_X"], np.float32).reshape(NCORES * T, D)
    blocks_all = [b for launch in LAUNCH_PLAN for b in launch]
    pre = host_prep(inp, blocks_all)
    state = _run_launches(_to_fm(x), pre, LAUNCH_PLAN)
    return _from_fm(state).reshape(4, SEQ, D)


# version 05
# speedup vs baseline: 1.0951x; 1.0951x over previous
import math
from contextlib import ExitStack

import numpy as np
import concourse.bass as bass
import concourse.mybir as mybir
from concourse.bass_utils import run_bass_kernel_spmd

F32 = mybir.dt.float32
BF16 = mybir.dt.bfloat16
AF = mybir.ActivationFunctionType
ALU = mybir.AluOpType
AX = mybir.AxisListType

NCORES = 8
T = 1024
SEQ = 2048
D = 2048
KC = 16
HID = 8192
HC = 64
NH = 16
MYH = 8
HD = 128
EPS = 1e-5

ALL_BLOCKS = [0, 1, 2, 3, 4, 5, 6, 7]
LAUNCH_PLAN = [ALL_BLOCKS]
DBG_G = None
DBG_AV = None

ENGS = ("tensor", "vector", "scalar", "gpsimd", "sync")
ESEM = {"tensor": "pe", "vector": "dve", "scalar": "act", "gpsimd": "pool",
        "sync": "sp"}
DMASEMS = ("dwa", "dwb", "dx")
CCSEM = "dcc"


PARANOID = False


class Planner:
    def __init__(self):
        self.ops = {e: [] for e in ENGS}
        self.semv = {}
        for s in list(ESEM.values()) + list(DMASEMS) + [CCSEM]:
            self.semv[s] = 0

    def add(self, eng, fn, waits=(), inc=None, tag=""):
        w = {}
        if PARANOID:
            for s, v in self.semv.items():
                if v > 0 and s != ESEM[eng]:
                    w[s] = v
        for tk in waits:
            if tk is None:
                continue
            s, v = tk
            w[s] = max(w.get(s, 0), v)
        tok = None
        amt = 0
        sem = None
        if inc is True:
            sem, amt = ESEM[eng], 1
        elif inc == CCSEM:
            sem, amt = inc, 1
        elif inc:
            sem, amt = inc, 16
        if sem is not None:
            self.semv[sem] += amt
            tok = (sem, self.semv[sem])
        self.ops[eng].append((fn, sorted(w.items()), (sem, amt), tag))
        return tok


class Ctx:
    def __init__(self, nc, plan):
        self.nc = nc
        self.p = plan
        self.t = {}
        self.uid = 0
        self.ps_readers = {0: [], 1: [], 2: [], 3: []}
        self.bufr = {}
        self.resid_ready = None
        self.ident_tok = None
        self.tri_tok = None
        self.roff_reg = None
        self.roff2_reg = None

    def nuid(self):
        self.uid += 1
        return self.uid

    def ps(self, pair, c0=0, c1=1024):
        return self.t["PS"][:, pair * 1024 + c0: pair * 1024 + c1]

    def ps_wait(self, pair):
        lst = self.ps_readers[pair]
        self.ps_readers[pair] = []
        return lst

    def ps_read(self, pair, tok):
        if tok:
            self.ps_readers[pair].append(tok)

    def buf_wait(self, name):
        lst = self.bufr.get(name, [])
        self.bufr[name] = []
        return lst

    def buf_read(self, name, tok):
        if tok:
            self.bufr.setdefault(name, []).append(tok)


def publish_dve(ctx, tag="pub"):
    return ctx.p.add("vector",
                     lambda e: e.tensor_copy(ctx.t["fence"][:, 16:32],
                                             ctx.t["fence"][:, 0:16]),
                     inc=True, tag=tag)



def host_prep(inp, blocks):
    pre = {}
    f32 = lambda a: np.asarray(a, dtype=np.float32)

    def rep(name, arr):
        pre[name] = [np.ascontiguousarray(arr)] * NCORES

    if 0 in blocks:
        win = f32(inp["L0_attn_a_W_in"])
        wout = f32(inp["L0_attn_a_W_out"])
        rep("w_a0_in", win.reshape(48, 128, KC, 128).transpose(0, 3, 2, 1)
            .reshape(48, 128, D))
        rep("w_a0_out", wout.reshape(KC, 128, KC, 128).transpose(0, 3, 2, 1)
            .reshape(KC, 128, D))
        bias = np.zeros((128, 64), np.float32)
        bias[:, 0:48] = f32(inp["L0_attn_a_b_in"]).reshape(48, 128).T
        bias[:, 48:64] = f32(inp["L0_attn_a_b_out"]).reshape(KC, 128).T
        rep("b_a0", bias)
        shuf = np.zeros((128, 128), np.float32)
        for k in range(64):
            shuf[2 * k + 1, 2 * k] = -1.0
            shuf[2 * k, 2 * k + 1] = 1.0
        rep("w_a0_shuf", shuf)
        inv = (10000.0 ** (-(np.arange(0, 1024, 2, dtype=np.float64) / 1024.0)))
        fidx = np.arange(1024) // 2
        trigs = []
        for c in range(NCORES):
            pos = (c % 2) * T + np.arange(T, dtype=np.float64)
            ang = pos[None, :] * inv[fidx][:, None]
            tab = np.stack([np.cos(ang), np.sin(ang)], axis=0)
            trigs.append(np.ascontiguousarray(
                tab.reshape(2, 8, 128, T).astype(np.float32)))
        pre["trig_a0"] = trigs

    if any(b in blocks for b in ATTN_BLOCKS):
        rep("ident", np.eye(128, dtype=np.float32))
        rep("tri", np.tril(np.ones((128, 128), np.float32)))
        pre["coremeta"] = [np.array([[(c % 2) * T, 0]], np.int32)
                           for c in range(NCORES)]

    if 2 in blocks:
        Ws = {w: f32(inp[f"L0_attn_b_W{w}"]) for w in "qkvo"}
        Bs = {w: f32(inp[f"L0_attn_b_B{w}"]) for w in "qkvo"}
        As = {w: f32(inp[f"L0_attn_b_A{w}"]) for w in "qkvo"}
        bs = {w: f32(inp[f"L0_attn_b_b{w}"]) for w in "qkvo"}
        qkv = np.zeros((NH, 3, 128, SLOTW), np.float32)
        for wi, w in enumerate("qkv"):
            sl = (Ws[w].reshape(NH, 128, KC, 128).transpose(0, 3, 2, 1)
                  .reshape(NH, 128, D))
            qkv[:, wi, :, 0:D] = sl
            at = As[w].T
            for gh in range(NH):
                qkv[gh, wi, wi * 32:wi * 32 + 16, D:D + 128] = \
                    at[:, gh * 128:(gh + 1) * 128]
        pre["w_a2_qkv"] = [np.ascontiguousarray(qkv[(c % 2) * 8:
                                                    (c % 2) * 8 + 8])
                           for c in range(NCORES)]
        wo2 = np.zeros((KC, 128, SLOTW), np.float32)
        wo2[:, :, 0:D] = (Ws["o"].reshape(KC, 128, KC, 128)
                          .transpose(0, 3, 2, 1).reshape(KC, 128, D))
        ao = As["o"].T
        for ft in range(KC):
            wo2[ft, 0:16, D:D + 128] = ao[:, ft * 128:(ft + 1) * 128]
        rep("w_a2_o", wo2)
        bt = np.zeros((128, 1024), np.float32)
        for wi, w in enumerate("qkv"):
            bT = Bs[w].T
            for kc in range(KC):
                bt[:, (wi * KC + kc) * 16:(wi * KC + kc) * 16 + 16] = \
                    bT[kc * 128:(kc + 1) * 128, :]
        boT = Bs["o"].T
        for kc in range(KC):
            bt[:, 768 + kc * 16:768 + kc * 16 + 16] = \
                boT[kc * 128:(kc + 1) * 128, :]
        rep("w_a2_B", bt)
        b2s = []
        for c in range(NCORES):
            rank = c % 2
            bias = np.zeros((128, 64), np.float32)
            for wi, w in enumerate("qkv"):
                for h in range(MYH):
                    g = rank * 8 + h
                    bias[:, wi * 8 + h] = bs[w][g * 128:(g + 1) * 128]
            bias[:, 48:64] = bs["o"].reshape(KC, 128).T
            b2s.append(np.ascontiguousarray(bias))
        pre["b_a2"] = b2s

    if 4 in blocks:
        wqkv = f32(inp["L1_attn_a_Wqkv"])
        bqkv = f32(inp["L1_attn_a_bqkv"])
        wo = f32(inp["L1_attn_a_Wo"])
        bo = f32(inp["L1_attn_a_bo"])
        w5 = (wqkv.reshape(KC, 128, 3, NH, 128).transpose(3, 2, 1, 0, 4)
              .reshape(NH, 3, 128, D))
        wos = (wo.reshape(KC, 128, KC, 128).transpose(2, 1, 0, 3)
               .reshape(KC, 128, D))
        pre["w_a4_qkv"] = [np.ascontiguousarray(w5[(c % 2) * 8:(c % 2) * 8 + 8])
                           for c in range(NCORES)]
        rep("w_a4_o", wos)
        b4s = []
        for c in range(NCORES):
            rank = c % 2
            bias = np.zeros((128, 64), np.float32)
            for w in range(3):
                for h in range(MYH):
                    g = rank * 8 + h
                    bias[:, w * 8 + h] = bqkv[w * 2048 + g * 128:
                                              w * 2048 + (g + 1) * 128]
            bias[:, 48:64] = bo.reshape(KC, 128).T
            b4s.append(np.ascontiguousarray(bias))
        pre["b_a4"] = b4s

    if 6 in blocks:
        def qkv_slabs(arr):
            return (arr.reshape(KC, 128, NH, 128).transpose(2, 1, 0, 3)
                    .reshape(NH, 128, D))

        cslab = np.stack([qkv_slabs(f32(inp[f"L1_attn_b_w{w}_codes"]))
                          for w in "qkv"], axis=1)
        sslab = np.stack(
            [qkv_slabs(np.repeat(f32(inp[f"L1_attn_b_w{w}_scales"]), 32,
                                 axis=0)) for w in "qkv"], axis=1)
        pre["w_a6_qkv_c"] = [np.ascontiguousarray(
            cslab[(c % 2) * 8:(c % 2) * 8 + 8]) for c in range(NCORES)]
        pre["w_a6_qkv_s"] = [np.ascontiguousarray(
            sslab[(c % 2) * 8:(c % 2) * 8 + 8]) for c in range(NCORES)]
        woc = f32(inp["L1_attn_b_wo_codes"])
        wosc = np.repeat(f32(inp["L1_attn_b_wo_scales"]), 32, axis=0)
        tr = lambda a: (a.reshape(KC, 128, KC, 128).transpose(2, 1, 0, 3)
                        .reshape(KC, 128, D))
        rep("w_a6_o_c", tr(woc))
        rep("w_a6_o_s", tr(wosc))
        b6s = []
        for c in range(NCORES):
            rank = c % 2
            bias = np.zeros((128, 64), np.float32)
            for wi, w in enumerate("qkv"):
                bb = f32(inp[f"L1_attn_b_b{w}"])
                for h in range(MYH):
                    g = rank * 8 + h
                    bias[:, wi * 8 + h] = bb[g * 128:(g + 1) * 128]
            bias[:, 48:64] = f32(inp["L1_attn_b_bo"]).reshape(KC, 128).T
            b6s.append(np.ascontiguousarray(bias))
        pre["b_a6"] = b6s

    if 1 in blocks or 3 in blocks:
        wg = f32(inp["L0_ffn_w_gate"])
        wu = f32(inp["L0_ffn_w_up"])
        wd = f32(inp["L0_ffn_w_down"])
        rep("w_f0_gate", wg.reshape(HC, 128, KC, 128).transpose(0, 3, 2, 1)
            .reshape(HC, 128, D))
        rep("w_f0_up", wu.reshape(HC, 128, KC, 128).transpose(0, 3, 2, 1)
            .reshape(HC, 128, D))
        rep("w_f0_down", wd.reshape(KC, 128, HC, 128).transpose(0, 2, 3, 1)
            .reshape(KC, 4, 16, 128, 128).transpose(0, 1, 3, 2, 4)
            .reshape(KC, 4, 128, 16 * 128))

    if 5 in blocks or 7 in blocks:
        wg = f32(inp["L1_ffn_W_gate"])
        wu = f32(inp["L1_ffn_W_up"])
        wd = f32(inp["L1_ffn_W_down"])
        rep("w_f1_gate", wg.reshape(HC, 128, KC, 128).transpose(0, 3, 2, 1)
            .reshape(HC, 128, D))
        rep("w_f1_up", wu.reshape(HC, 128, KC, 128).transpose(0, 3, 2, 1)
            .reshape(HC, 128, D))
        rep("w_f1_down", wd.reshape(KC, 128, HC, 128).transpose(0, 2, 3, 1)
            .reshape(KC, 4, 16, 128, 128).transpose(0, 1, 3, 2, 4)
            .reshape(KC, 4, 128, 16 * 128))
        v = np.zeros((128, 224), np.float32)
        v[:, 0:16] = f32(inp["L1_ffn_SU_ug"]).reshape(16, 128).T
        v[:, 16:80] = f32(inp["L1_ffn_SU_dp"]).reshape(64, 128).T
        wsug = f32(inp["L1_ffn_Wscale_ug"])
        v[:, 80:144] = wsug[:8192].reshape(64, 128).T
        v[:, 144:208] = wsug[8192:].reshape(64, 128).T
        v[:, 208:224] = f32(inp["L1_ffn_Wscale_dp"]).reshape(16, 128).T
        rep("v_f1", v)
        ii, jj = np.meshgrid(np.arange(128), np.arange(128), indexing="ij")
        h128 = np.where(
            np.bitwise_count(np.bitwise_and(ii, jj)) % 2 == 0, 1.0, -1.0)
        rep("h128", h128.astype(np.float32))
    return pre



ATTN_BLOCKS = (2, 4, 6)


def declare_ios(ctx, blocks):
    nc = ctx.nc
    names = []
    ctx.t["rin"] = nc.dram_tensor("rin", [KC, 128, T], F32, kind="ExternalInput")
    ctx.t["rout"] = nc.dram_tensor("rout", [KC, 128, T], F32, kind="ExternalOutput")

    def din(name, shape, dtype=F32):
        ctx.t[name] = nc.dram_tensor(name, shape, dtype, kind="ExternalInput")
        names.append(name)

    if any(b in blocks for b in ATTN_BLOCKS):
        din("ident", [128, 128])
        din("tri", [128, 128])
        din("coremeta", [1, 2], mybir.dt.int32)
        ctx.t["x1c"] = nc.dram_tensor("x1c", [2048, T], BF16)
        ctx.t["x1g"] = nc.dram_tensor("x1g", [4096, T], BF16)
        ctx.t["aoc"] = nc.dram_tensor("aoc", [1024, SEQ], BF16)
        ctx.t["aog"] = nc.dram_tensor("aog", [2048, SEQ], BF16)
    if 2 in blocks:
        din("w_a2_qkv", [MYH, 3, 128, SLOTW])
        din("w_a2_o", [KC, 128, SLOTW])
        din("w_a2_B", [128, 1024])
        din("b_a2", [128, 64])
    if 4 in blocks:
        din("w_a4_qkv", [MYH, 3, 128, D])
        din("w_a4_o", [KC, 128, D])
        din("b_a4", [128, 64])
    if 6 in blocks:
        din("w_a6_qkv_c", [MYH, 3, 128, D])
        din("w_a6_qkv_s", [MYH, 3, 128, D])
        din("w_a6_o_c", [KC, 128, D])
        din("w_a6_o_s", [KC, 128, D])
        din("b_a6", [128, 64])

    if 0 in blocks:
        din("w_a0_in", [48, 128, D])
        din("w_a0_out", [KC, 128, D])
        din("b_a0", [128, 64])
        din("w_a0_shuf", [128, 128])
        din("trig_a0", [2, 8, 128, T])
        ctx.t["cc_a_in"] = nc.dram_tensor("cc_a_in", [128, 16], F32)
        ctx.t["cc_a_out"] = nc.dram_tensor("cc_a_out", [128, 16], F32)
    if 1 in blocks or 3 in blocks:
        din("w_f0_gate", [HC, 128, D])
        din("w_f0_up", [HC, 128, D])
        din("w_f0_down", [KC, 4, 128, 16 * 128])
    if 5 in blocks or 7 in blocks:
        din("w_f1_gate", [HC, 128, D])
        din("w_f1_up", [HC, 128, D])
        din("w_f1_down", [KC, 4, 128, 16 * 128])
        din("v_f1", [128, 224])
        din("h128", [128, 128])
    return names


def alloc_onchip(ctx, es):
    nc = ctx.nc
    t = ctx.t

    def sb(name, shape, dtype):
        t[name] = es.enter_context(nc.sbuf_tensor(name, shape, dtype))

    def ps(name, shape, dtype):
        t[name] = es.enter_context(nc.psum_tensor(name, shape, dtype))

    sb("R", [128, KC * T], F32)
    sb("lnb", [128, KC * T], BF16)
    sb("a1b", [128, KC * T], BF16)
    sb("a2", [128, KC * T], BF16)
    sb("wst", [128, 2 * 2176], F32)
    sb("wbf", [128, 2 * 2176], BF16)
    sb("blor", [128, 1024], BF16)
    sb("sq", [128, 2048], BF16)
    sb("strips", [128, 3072], F32)
    sb("trig", [128, 2048], F32)
    sb("rsc", [128, 1024], BF16)
    sb("amine", [128, 16], F32)
    sb("afull", [128, 16], F32)
    sb("shufst", [128, 128], F32)
    sb("shufbf", [128, 128], BF16)
    sb("identbf", [128, 128], BF16)
    sb("tribf", [128, 128], BF16)
    sb("negm", [128, 4], F32)
    sb("rsn", [128, 4], F32)
    sb("fence", [128, 32], BF16)
    sb("ul", [128, 2048], BF16)
    sb("suf", [128, 224], F32)
    sb("h128bf", [128, 128], BF16)
    sb("cmsb", [1, 2], mybir.dt.int32)
    sb("bias", [128, 64], F32)
    sb("onec", [128, 1], BF16)
    sb("oner", [1, 128], F32)

    ps("PS", [128, 3072], F32)
    ps("PSB", [128, 2048], BF16)


def emit_setup(ctx):
    p, t = ctx.p, ctx.t
    tok = p.add("sync",
                lambda e: e.dma_start(
                    out=t["R"].ap().rearrange("p (k t) -> p k t", k=KC),
                    in_=t["rin"].ap().rearrange("k p t -> p k t")),
                inc="dx", tag="load_resid")
    ctx.resid_ready = tok
    t1 = p.add("gpsimd", lambda e: e.memset(t["onec"][:, :], 1.0), inc=True)
    t2 = p.add("gpsimd", lambda e: e.memset(t["oner"][:, :], 1.0), inc=True)
    ctx.init_done = t2


def emit_attn_consts(ctx):
    p, t = ctx.p, ctx.t
    cm = p.add("sync", lambda e: e.dma_start(out=t["cmsb"][:, :],
                                             in_=t["coremeta"].ap()),
               inc="dx", tag="cm_dma")
    p.add("sync", lambda e: e.reg_load(ctx.roff_reg, t["cmsb"][0:1, 0:1]),
          waits=[cm], tag="cm_load")
    p.add("sync", lambda e: e.reg_add(ctx.roff2_reg, ctx.roff_reg,
                                      8 * 128 * SEQ), tag="cm_add")
    i1 = p.add("sync", lambda e: e.dma_start(out=t["shufst"][:, :],
                                             in_=t["ident"].ap()),
               waits=ctx.buf_wait("shufst"), inc="dx", tag="id_dma")
    ic = p.add("gpsimd", lambda e: e.tensor_copy(t["identbf"][:, :],
                                                 t["shufst"][:, :]),
               waits=[i1], inc=True, tag="id_cast")
    ctx.buf_read("shufst", ic)
    t1 = p.add("sync", lambda e: e.dma_start(out=t["shufst"][:, :],
                                             in_=t["tri"].ap()),
               waits=ctx.buf_wait("shufst"), inc="dx", tag="tri_dma")
    tc = p.add("gpsimd", lambda e: e.tensor_copy(t["tribf"][:, :],
                                                 t["shufst"][:, :]),
               waits=[t1], inc=True, tag="tri_cast")
    ctx.buf_read("shufst", tc)
    ctx.ident_tok = ic
    ctx.tri_tok = tc


def emit_store(ctx):
    p, t = ctx.p, ctx.t
    tok = p.add("sync",
                lambda e: e.dma_start(
                    out=t["rout"].ap().rearrange("k p t -> p k t"),
                    in_=t["R"].ap().rearrange("p (k t) -> p k t", k=KC)),
                waits=[ctx.resid_ready], inc="dx", tag="store_resid")
    p.add("sync", lambda e: e.nop(), waits=[tok], tag="final_wait")


def emit_ln(ctx):
    nc, p, t = ctx.nc, ctx.p, ctx.t
    u = ctx.nuid()
    lnb, R, sq = t["lnb"], t["R"], t["sq"]
    strips = t["strips"]

    cast_tok = p.add("scalar",
                     lambda e: e.copy(lnb[:, :], R[:, :]),
                     waits=[ctx.resid_ready, ctx.init_done]
                           + ctx.buf_wait("lnb"),
                     inc=True, tag=f"ln{u}_cast")

    sr = strips[0:1, 0:1024]
    sm = strips[0:1, 1024:2048]
    sv = strips[0:1, 2048:3072]
    smr = sv
    s1 = t["PS"][0:1, 0:1024]
    s2 = t["PS"][0:1, 1024:2048]
    RB = ctx.ps(1)
    MRB = ctx.ps(2)

    w0 = ctx.ps_wait(0) + ctx.ps_wait(1) + ctx.ps_wait(2)
    sq_toks = []
    for c in range(KC):
        cc = c % 2
        sqw = [cast_tok]
        if c >= 2:
            sqw.append(sq_toks[c - 2][1])
        sq_tok = p.add("scalar",
                       lambda e, c=c, cc=cc: e.activation(
                           sq[:, cc * 1024:(cc + 1) * 1024],
                           lnb[:, c * T:(c + 1) * T], AF.Square),
                       waits=sqw, inc=True, tag=f"ln{u}_sq{c}")
        for h in range(2):
            p.add("tensor",
                  lambda e, c=c, h=h: e.matmul(
                      s1[:, h * 512:(h + 1) * 512], t["onec"][:, :],
                      lnb[:, c * T + h * 512:c * T + (h + 1) * 512],
                      start=(c == 0), stop=(c == KC - 1)),
                  waits=([cast_tok] + w0) if c == 0 and h == 0 else [],
                  tag=f"ln{u}_s1_{c}_{h}")
        pe_tok = None
        for h in range(2):
            pe_tok = p.add("tensor",
                           lambda e, c=c, h=h, cc=cc: e.matmul(
                               s2[:, h * 512:(h + 1) * 512], t["onec"][:, :],
                               sq[:, cc * 1024 + h * 512:cc * 1024 + (h + 1) * 512],
                               start=(c == 0), stop=(c == KC - 1)),
                           waits=[sq_tok] if h == 0 else [],
                           inc=True if h == 1 else None,
                           tag=f"ln{u}_s2_{c}_{h}")
        sq_toks.append((sq_tok, pe_tok))
    stats_done = sq_toks[-1][1]

    v1 = p.add("vector", lambda e: e.tensor_scalar_mul(sm, s1, 1.0 / D),
               waits=[stats_done], tag=f"ln{u}_m")
    p.add("vector", lambda e: e.tensor_scalar_mul(sv, s2, 1.0 / D))
    p.add("vector", lambda e: e.tensor_tensor(sr, sm, sm, ALU.mult))
    p.add("vector", lambda e: e.tensor_tensor(sv, sv, sr, ALU.subtract))
    var_tok = p.add("vector", lambda e: e.tensor_scalar_add(sv, sv, EPS),
                    inc=True, tag=f"ln{u}_var")
    ctx.ps_read(0, var_tok)
    ctx.ps_read(1, var_tok)
    sqrt_tok = p.add("scalar", lambda e: e.activation(sr, sv, AF.Sqrt),
                     waits=[var_tok], inc=True, tag=f"ln{u}_sqrt")
    rcp_tok = p.add("vector", lambda e: e.reciprocal(sr, sr),
                    waits=[sqrt_tok], inc=True, tag=f"ln{u}_rstd")
    mr_tok = p.add("vector", lambda e: e.tensor_tensor(smr, sm, sr, ALU.mult),
                   inc=True, tag=f"ln{u}_mr")

    bc_tok = None
    for h in range(2):
        p.add("tensor",
              lambda e, h=h: e.matmul(RB[:, h * 512:(h + 1) * 512],
                                      t["oner"][:, :],
                                      sr[:, h * 512:(h + 1) * 512],
                                      start=True, stop=True),
              waits=[rcp_tok, mr_tok] if h == 0 else [], tag=f"ln{u}_bcR{h}")
    for h in range(2):
        bc_tok = p.add("tensor",
                       lambda e, h=h: e.matmul(MRB[:, h * 512:(h + 1) * 512],
                                               t["oner"][:, :],
                                               smr[:, h * 512:(h + 1) * 512],
                                               start=True, stop=True),
                       inc=True if h == 1 else None, tag=f"ln{u}_bcM{h}")

    norm_tok = None
    for c in range(KC):
        p.add("vector",
              lambda e, c=c: e.tensor_tensor(lnb[:, c * T:(c + 1) * T],
                                             lnb[:, c * T:(c + 1) * T],
                                             RB, ALU.mult),
              waits=[bc_tok] if c == 0 else [], tag=f"ln{u}_n1_{c}")
        norm_tok = p.add("vector",
                         lambda e, c=c: e.tensor_tensor(
                             lnb[:, c * T:(c + 1) * T],
                             lnb[:, c * T:(c + 1) * T], MRB, ALU.subtract),
                         inc=True if c == KC - 1 else None,
                         tag=f"ln{u}_n2_{c}")
    ctx.ps_read(1, norm_tok)
    ctx.ps_read(2, norm_tok)
    return publish_dve(ctx, f"ln{u}_pub")


SLOTW = 2176


def stream_weight(ctx, dram_ap, slot, queue, tag, scale=None, width=2048,
                  cast_eng="scalar"):
    p, t = ctx.p, ctx.t
    c0 = slot * SLOTW
    d_tok = p.add(queue,
                  lambda e: e.dma_start(out=t["wst"][:, c0:c0 + width],
                                        in_=dram_ap),
                  waits=ctx.buf_wait(("wst", slot)),
                  inc="dwa" if queue == "sync" else "dwb", tag=f"dma_{tag}")
    if cast_eng == "scalar":
        if scale is None:
            fn = lambda e: e.copy(t["wbf"][:, c0:c0 + width],
                                  t["wst"][:, c0:c0 + width])
        else:
            fn = lambda e: e.mul(t["wbf"][:, c0:c0 + width],
                                 t["wst"][:, c0:c0 + width], scale)
    elif scale is None:
        fn = lambda e: e.tensor_copy(t["wbf"][:, c0:c0 + width],
                                     t["wst"][:, c0:c0 + width])
    else:
        fn = lambda e: e.tensor_scalar_mul(t["wbf"][:, c0:c0 + width],
                                           t["wst"][:, c0:c0 + width], scale)
    cast_tok = p.add(cast_eng, fn,
                     waits=[d_tok] + ctx.buf_wait(("wbf", slot)),
                     inc=True, tag=f"cast_{tag}")
    ctx.buf_read(("wst", slot), cast_tok)
    if cast_eng == "vector":
        cast_tok = publish_dve(ctx, f"castp_{tag}")
    return cast_tok, c0


def emit_ffn0(ctx):
    nc, p, t = ctx.nc, ctx.p, ctx.t
    u = ctx.nuid()
    lnb = t["lnb"]
    norm_done = emit_ln(ctx)

    wg_d = t["w_f0_gate"].ap()
    wu_d = t["w_f0_up"].ap()
    wd_d = t["w_f0_down"].ap()
    wbf = t["wbf"]

    def g_ap(i, c0=0, c1=T):
        base = t["a2"] if i < 16 else t["a1b"]
        off = (i % 16) * T
        return base[:, off + c0:off + c1]

    r_tok = None
    for half in range(2):
        g_toks = []
        for i in range(32):
            hc = half * 32 + i
            cg, bg = stream_weight(ctx, wg_d[hc], 0, "sync", f"f{u}g{hc}", cast_eng="gpsimd")
            cu, bu = stream_weight(ctx, wu_d[hc], 1, "scalar", f"f{u}u{hc}", cast_eng="vector")
            pg = (2 * i) % 3
            pu = (2 * i + 1) % 3
            gate_ps = ctx.ps(pg)
            up_ps = ctx.ps(pu)
            mw = [cg, norm_done] + ctx.ps_wait(pg)
            gate_tail = None
            for kc in range(KC):
                for h in range(2):
                    gate_tail = p.add(
                        "tensor",
                        lambda e, kc=kc, h=h, gate_ps=gate_ps, bg=bg: e.matmul(
                            gate_ps[:, h * 512:(h + 1) * 512],
                            wbf[:, bg + kc * 128:bg + (kc + 1) * 128],
                            lnb[:, kc * T + h * 512:kc * T + (h + 1) * 512],
                            start=(kc == 0), stop=(kc == KC - 1)),
                        waits=mw if kc == 0 and h == 0 else [],
                        inc=True if kc == KC - 1 and h == 1 else None,
                        tag=f"f{u}_gmm{hc}")
            mw = [cu] + ctx.ps_wait(pu)
            up_tail = None
            for kc in range(KC):
                for h in range(2):
                    up_tail = p.add(
                        "tensor",
                        lambda e, kc=kc, h=h, up_ps=up_ps, bu=bu: e.matmul(
                            up_ps[:, h * 512:(h + 1) * 512],
                            wbf[:, bu + kc * 128:bu + (kc + 1) * 128],
                            lnb[:, kc * T + h * 512:kc * T + (h + 1) * 512],
                            start=(kc == 0), stop=(kc == KC - 1)),
                        waits=mw if kc == 0 and h == 0 else [],
                        inc=True if kc == KC - 1 and h == 1 else None,
                        tag=f"f{u}_umm{hc}")
            ctx.buf_read(("wbf", 0), gate_tail)
            ctx.buf_read(("wbf", 1), up_tail)
            cc = i % 2
            sg = t["sq"][:, cc * 1024:(cc + 1) * 1024]
            silu_tok = p.add("scalar",
                             lambda e, sg=sg, gate_ps=gate_ps: e.activation(
                                 sg, gate_ps, AF.Silu),
                             waits=[gate_tail] + ctx.buf_wait(("sq", cc)),
                             inc=True, tag=f"f{u}_silu{hc}")
            ctx.ps_read(pg, silu_tok)
            g_tok = p.add("vector",
                          lambda e, i=i, sg=sg, up_ps=up_ps: e.tensor_tensor(
                              g_ap(i), sg, up_ps, ALU.mult),
                          waits=[up_tail, silu_tok] + ctx.buf_wait(("g", i)),
                          inc=True, tag=f"f{u}_g{hc}")
            ctx.ps_read(pu, g_tok)
            ctx.buf_read(("sq", cc), g_tok)
            g_toks.append(g_tok)
        all_g = publish_dve(ctx, f"f{u}_gpub{half}")

        if DBG_G == half:
            t["gdump"] = nc.dram_tensor("gdump", [32, 128, T], BF16,
                                        kind="ExternalOutput")
            tok1 = p.add("sync", lambda e: e.dma_start(
                out=t["gdump"].ap()[0:16].rearrange("k p t -> p k t"),
                in_=t["a2"].ap().rearrange("p (k t) -> p k t", k=16)),
                waits=[all_g], inc="dx")
            tok2 = p.add("sync", lambda e: e.dma_start(
                out=t["gdump"].ap()[16:32].rearrange("k p t -> p k t"),
                in_=t["a1b"].ap().rearrange("p (k t) -> p k t", k=16)),
                waits=[all_g], inc="dx")
            p.add("sync", lambda e: e.nop(), waits=[tok2])
            ctx.resid_ready = ctx.resid_ready
            return all_g

        for ft in range(KC):
            pd = (2 * 32 + ft) % 3
            dn_ps = ctx.ps(pd)
            mm_tail = None
            for sl in range(2):
                hgrp = half * 2 + sl
                cd, bd = stream_weight(ctx, wd_d[ft, hgrp], sl,
                                       "sync" if sl == 0 else "scalar",
                                       f"f{u}d{half}_{ft}_{sl}",
                                       cast_eng="vector")
                mw = [cd, all_g] + (ctx.ps_wait(pd) if sl == 0 else [])
                for i in range(16):
                    gi = sl * 16 + i
                    for h in range(2):
                        mm_tail = p.add(
                            "tensor",
                            lambda e, i=i, h=h, gi=gi, bd=bd, dn_ps=dn_ps, sl=sl:
                            e.matmul(
                                dn_ps[:, h * 512:(h + 1) * 512],
                                wbf[:, bd + i * 128:bd + (i + 1) * 128],
                                g_ap(gi, h * 512, (h + 1) * 512),
                                start=(sl == 0 and i == 0),
                                stop=(sl == 1 and i == 15)),
                            waits=mw if i == 0 and h == 0 else [],
                            inc=True if sl == 1 and i == 15 and h == 1 else None,
                            tag=f"f{u}_dmm{half}_{ft}")
                ctx.buf_read(("wbf", sl), mm_tail)
            r_tok = p.add("vector",
                          lambda e, ft=ft, dn_ps=dn_ps: e.scalar_tensor_tensor(
                              t["R"][:, ft * T:(ft + 1) * T], dn_ps, 0.0,
                              t["R"][:, ft * T:(ft + 1) * T],
                              ALU.add, ALU.add),
                          waits=[mm_tail], inc=True, tag=f"f{u}_racc{half}_{ft}")
            ctx.ps_read(pd, r_tok)
        if half == 0:
            for i in range(32):
                ctx.buf_read(("g", i), r_tok)
    ctx.buf_read("lnb", r_tok)
    ctx.resid_ready = r_tok
    return r_tok


def emit_softmax_attn(ctx, cfg):
    nc, p, t = ctx.nc, ctx.p, ctx.t
    u = ctx.nuid()
    lnb = t["lnb"]
    norm_done = emit_ln(ctx)

    d_out = p.add("sync",
                  lambda e: e.dma_start(
                      out=t["x1c"].ap().rearrange("(k p) t -> p k t", p=128),
                      in_=lnb.ap().rearrange("p (k t) -> p k t", k=KC)),
                  waits=[norm_done], inc="dx", tag=f"s{u}_lndump")
    cc1 = p.add("gpsimd",
                lambda e: e.collective_compute(
                    "AllGather", ALU.bypass,
                    replica_groups=[[0, 1], [2, 3], [4, 5], [6, 7]],
                    ins=[t["x1c"].ap().opt()], outs=[t["x1g"].ap().opt()]),
                waits=[d_out], inc=CCSEM, tag=f"s{u}_lncc")
    rb = []
    for r in range(2):
        for half in range(2):
            arena = t["a1b"] if half == 0 else t["a2"]
            rb.append(p.add(
                "sync",
                lambda e, r=r, half=half, arena=arena: e.dma_start(
                    out=arena.ap().rearrange("p (k t) -> p k t", k=8)
                        [:, :, r * T:(r + 1) * T],
                    in_=t["x1g"].ap()
                        .rearrange("(r k p) t -> r p k t", r=2, p=128)
                        [r, :, half * 8:(half + 1) * 8, :]),
                waits=[cc1], inc="dx", tag=f"s{u}_lnrb{r}{half}"))
    lnf_ready = rb[-1]

    def lnf(kc):
        arena = t["a1b"] if kc < 8 else t["a2"]
        return arena[:, (kc % 8) * SEQ:(kc % 8) * SEQ + SEQ]

    pre_tok = None
    if cfg.get("pre_heads"):
        pre_tok = cfg["pre_heads"](lnf, lnf_ready)

    qh = lnb[:, 0:2048]
    kh = lnb[:, 2048:4096]
    vh = lnb[:, 4096:6144]
    vt = lnb[:, 6144:6144 + 16 * 129]
    attT = lnb[:, 8256:8256 + 2048]
    oh = lnb[:, 10304:10304 + 2048]
    attn = t["sq"]
    outq = lnb[:, 12352:12352 + 128]
    negm = t["negm"]
    SC = t["PS"][:, 0:2048]
    AVP = [ctx.ps(2, 0, 512), ctx.ps(2, 512, 1024)]

    vt_init = p.add("gpsimd", lambda e: e.memset(vt, 1.0),
                    waits=[d_out], inc=True, tag=f"s{u}_vt1")

    dst_map = {0: qh, 1: kh, 2: vh}
    slotc = [0]

    def next_slot():
        s = slotc[0] % 2
        q = "sync" if s == 0 else "scalar"
        slotc[0] += 1
        ctx._cur_wslot = s
        return s, q

    r_tok = None
    oh_dma = []
    for h in range(MYH):
        ev_toks = {}
        for w in range(3):
            slot, queue = next_slot()
            wtok, base = cfg["make_w"](h, w, slot, queue, f"s{u}_w{h}_{w}")
            pr01 = ctx.ps_wait(0) + ctx.ps_wait(1)
            tail = None
            closes = not cfg["has_extra_w"]
            for kc in range(KC):
                for q4 in range(4):
                    tail = p.add(
                        "tensor",
                        lambda e, kc=kc, q4=q4, base=base, closes=closes:
                        e.matmul(
                            SC[:, q4 * 512:(q4 + 1) * 512],
                            t["wbf"][:, base + kc * 128:base + (kc + 1) * 128],
                            lnf(kc)[:, q4 * 512:(q4 + 1) * 512],
                            start=(kc == 0), stop=(closes and kc == KC - 1)),
                        waits=([wtok, lnf_ready, pre_tok] + pr01)
                              if kc == 0 and q4 == 0 else [],
                        inc=True if closes and kc == KC - 1 and q4 == 3
                        else None,
                        tag=f"s{u}_p{h}_{w}")
            etok = cfg["extra_mm"](h, w, SC, f"s{u}_x{h}_{w}")
            ctx.buf_read(("wbf", ctx._cur_wslot), etok or tail)
            bias = cfg["bias_col"](w, h)
            dst = dst_map[w]
            ev = p.add("scalar",
                       lambda e, dst=dst, bias=bias: e.activation(
                           dst, SC, AF.Identity,
                           bias=bias if bias is not None else 0.0),
                       waits=[etok or tail, cfg["bias_tok"]], inc=True,
                       tag=f"s{u}_ev{h}_{w}")
            ctx.ps_read(0, ev)
            ctx.ps_read(1, ev)
            ev_toks[w] = ev
        tp_toks = []
        pw3 = ctx.ps_wait(3)
        for kc in range(KC):
            tp = p.add("tensor",
                       lambda e, kc=kc: e.transpose(
                           t["PSB"][:, kc * 128:(kc + 1) * 128],
                           vh[:, kc * 128:(kc + 1) * 128], t["identbf"][:, :]),
                       waits=([ev_toks[2], vt_init, ctx.ident_tok] + pw3)
                             if kc == 0 else [],
                       inc=True if kc == KC - 1 else None, tag=f"s{u}_vt{h}")
            tp_toks.append(tp)
        vt_ev = p.add("vector",
                      lambda e: e.tensor_copy(
                          vt.rearrange("p (k c) -> p k c", c=129)[:, :, 0:128],
                          t["PSB"][:, :].rearrange("p (k c) -> p k c", c=128)),
                      waits=[tp_toks[-1]], inc=True, tag=f"s{u}_vte{h}")
        ctx.ps_read(3, vt_ev)
        vt_ev = publish_dve(ctx, f"s{u}_vtp{h}")

        nqt = 16
        for qt in range(nqt):
            Wk = (qt + 1) * 128 if cfg["causal"] else 2048
            nkc = Wk // 128
            sc_w = ctx.ps_wait(0) + ctx.ps_wait(1)
            sc_tail = None
            nmm = (Wk + 511) // 512
            for i in range(nmm):
                c0, c1 = i * 512, min(Wk, (i + 1) * 512)
                sc_tail = p.add(
                    "tensor",
                    lambda e, c0=c0, c1=c1, qt=qt: e.matmul(
                        SC[:, c0:c1], qh[:, qt * 128:(qt + 1) * 128],
                        kh[:, c0:c1], start=True, stop=True),
                    waits=([ev_toks[0], ev_toks[1]] + sc_w) if i == 0 else [],
                    inc=True if i == nmm - 1 else None, tag=f"s{u}_sc{h}_{qt}")
            nslot = negm[:, qt % 2:qt % 2 + 1]
            mxs = p.add("vector",
                        lambda e, nslot=nslot, Wk=Wk: e.reduce_max(
                            nslot, SC[:, 0:Wk], AX.X, negate=True),
                        waits=[sc_tail] + ctx.buf_wait(("negm", qt % 2)),
                        inc=True, tag=f"s{u}_mx{h}_{qt}")
            ex = p.add("scalar",
                       lambda e, nslot=nslot, Wk=Wk: e.activation(
                           attn[:, 0:Wk], SC[:, 0:Wk], AF.Exp,
                           bias=nslot),
                       waits=[mxs] + ctx.buf_wait(("attn",)), inc=True,
                       tag=f"s{u}_ex{h}_{qt}")
            ctx.buf_read(("negm", qt % 2), ex)
            ctx.ps_read(0, ex)
            ctx.ps_read(1, ex)
            em = ex
            if cfg["causal"]:
                p.add("vector",
                      lambda e, qt=qt: e.tensor_tensor(
                          attn[:, qt * 128:(qt + 1) * 128],
                          attn[:, qt * 128:(qt + 1) * 128],
                          t["tribf"][:, :], ALU.mult),
                      waits=[ex, ctx.tri_tok], inc=True,
                      tag=f"s{u}_msk{h}_{qt}")
                em = publish_dve(ctx, f"s{u}_mskp{h}_{qt}")
            pw3 = ctx.ps_wait(3)
            tp_tail = None
            for kc in range(nkc):
                tp_tail = p.add(
                    "tensor",
                    lambda e, kc=kc: e.transpose(
                        t["PSB"][:, kc * 128:(kc + 1) * 128],
                        attn[:, kc * 128:(kc + 1) * 128], t["identbf"][:, :]),
                    waits=([em] + pw3) if kc == 0 else [],
                    inc=True if kc == nkc - 1 else None,
                    tag=f"s{u}_at{h}_{qt}")
            at_ev = p.add("vector",
                          lambda e, Wk=Wk: e.tensor_copy(attT[:, 0:Wk],
                                                         t["PSB"][:, 0:Wk]),
                          waits=[tp_tail], inc=True, tag=f"s{u}_ae{h}_{qt}")
            ctx.ps_read(3, at_ev)
            ctx.buf_read(("attn",), at_ev)
            at_ev = publish_dve(ctx, f"s{u}_aep{h}_{qt}")
            avp = AVP[qt % 2]
            av_w = ctx.ps_wait(2) if qt < 2 else []
            av_tail = None
            for kc in range(nkc):
                av_tail = p.add(
                    "tensor",
                    lambda e, kc=kc, avp=avp: e.matmul(
                        avp[:, 0:129], attT[:, kc * 128:(kc + 1) * 128],
                        vt[:, kc * 129:(kc + 1) * 129],
                        start=(kc == 0), stop=(kc == nkc - 1)),
                    waits=([at_ev, vt_ev] + av_w +
                           ctx.buf_wait(("avp", qt % 2))) if kc == 0 else [],
                    inc=True if kc == nkc - 1 else None, tag=f"s{u}_av{h}_{qt}")
            if DBG_AV is not None and h == DBG_AV[0] and qt == DBG_AV[1]:
                t["avdump"] = nc.dram_tensor("avdump", [128, 2048], F32,
                                             kind="ExternalOutput")
                t["avdump2"] = nc.dram_tensor("avdump2", [128, 256], F32,
                                              kind="ExternalOutput")
                stg = t["trig"]
                dtok1 = p.add("vector",
                              lambda e, avp=avp: e.tensor_copy(
                                  stg[:, 0:512], avp),
                              waits=[av_tail], inc=True, tag="dbgav1")
                dtok2 = p.add("vector",
                              lambda e, nslot=nslot: e.tensor_copy(
                                  stg[:, 512:513], nslot),
                              waits=[mxs], inc=True, tag="dbgav2")
                dtok3 = p.add("vector",
                              lambda e: e.tensor_copy(
                                  stg[:, 640:768], attn[:, 0:128]),
                              waits=[em], inc=True, tag="dbgav2b")
                dtok4 = p.add("vector",
                              lambda e: e.tensor_copy(
                                  stg[:, 768:896], SC[:, 0:128]),
                              waits=[sc_tail], inc=True, tag="dbgav2c")
                ctx.ps_read(0, dtok4)
                ctx.buf_read(("avp", qt % 2), dtok1)
                ctx.buf_read(("attn",), dtok3)
                dd = p.add("sync",
                           lambda e: e.dma_start(out=t["avdump"].ap(),
                                                 in_=stg[:, :]),
                           waits=[dtok1, dtok2, dtok3, dtok4], inc="dx",
                           tag="dbgav3")
                p.add("sync", lambda e: e.nop(), waits=[dd])
            rs = t["rsn"][:, qt % 4:qt % 4 + 1]
            rc = p.add("vector",
                       lambda e, rs=rs, avp=avp: e.reciprocal(
                           rs, avp[:, 128:129]),
                       waits=[av_tail], tag=f"s{u}_rs{h}_{qt}")
            p.add("vector", lambda e: e.tensor_copy(t["fence"][:, :],
                                                    t["tribf"][:, 0:32]),
                  tag=f"s{u}_fn{h}_{qt}")
            nq = p.add("vector",
                       lambda e, rs=rs, avp=avp: e.tensor_scalar_mul(
                           outq, avp[:, 0:128], rs),
                       waits=ctx.buf_wait(("outq",)), inc=True,
                       tag=f"s{u}_nq{h}_{qt}")
            ctx.buf_read(("avp", qt % 2), nq)
            nq = publish_dve(ctx, f"s{u}_nqp{h}_{qt}")
            if DBG_AV is not None and h == DBG_AV[0] and qt == DBG_AV[1]:
                stg = t["trig"]
                d3 = p.add("vector",
                           lambda e, rs=rs: e.tensor_copy(stg[:, 1536:1537],
                                                          rs),
                           waits=[nq], inc=True, tag="dbgav4")
                d4 = p.add("vector",
                           lambda e: e.tensor_copy(stg[:, 1600:1728], outq),
                           inc=True, tag="dbgav5")
                dd2 = p.add("sync",
                            lambda e: e.dma_start(
                                out=t["avdump2"].ap(), in_=stg[:, 1536:1792]),
                            waits=[d4], inc="dx", tag="dbgav6")
                p.add("sync", lambda e: e.nop(), waits=[dd2])
            pw3 = ctx.ps_wait(3)
            ot = p.add("tensor",
                       lambda e: e.transpose(t["PSB"][:, 1920:2048], outq,
                                             t["identbf"][:, :]),
                       waits=[nq] + pw3, inc=True, tag=f"s{u}_ot{h}_{qt}")
            oe = p.add("scalar",
                       lambda e, qt=qt: e.copy(oh[:, qt * 128:(qt + 1) * 128],
                                               t["PSB"][:, 1920:2048]),
                       waits=[ot] + ctx.buf_wait(("oh", qt)), inc=True,
                       tag=f"s{u}_oe{h}_{qt}")
            ctx.ps_read(3, oe)
            ctx.buf_read(("outq",), ot)
            ctx.buf_read(("oh", qt), None)
        ohd = p.add("sync",
                    lambda e, h=h: e.dma_start(
                        out=t["aoc"].ap()[h * 128:(h + 1) * 128, :],
                        in_=oh),
                    waits=[oe], inc="dx", tag=f"s{u}_ohd{h}")
        oh_dma.append(ohd)
        for qt in range(16):
            ctx.buf_read(("oh", qt), ohd)

    ctx.ps_read(2, nq)

    cc2 = p.add("gpsimd",
                lambda e: e.collective_compute(
                    "AllGather", ALU.bypass,
                    replica_groups=[[0, 1], [2, 3], [4, 5], [6, 7]],
                    ins=[t["aoc"].ap().opt()], outs=[t["aog"].ap().opt()]),
                waits=[oh_dma[-1]], inc=CCSEM, tag=f"s{u}_aocc")
    rb2 = []
    for half in range(2):
        arena = t["a1b"] if half == 0 else t["a2"]
        roff = ctx.roff_reg if half == 0 else ctx.roff2_reg
        rb2.append(p.add(
            "sync",
            lambda e, half=half, arena=arena, roff=roff: e.dma_start(
                out=arena.ap()[:, 0:8 * T].rearrange("p (k t) -> p k t", k=8),
                in_=bass.AP(t["aog"], roff,
                            [[SEQ, 128], [128 * SEQ, 8], [1, T]],
                            dep_tracking_offset=half * 8 * 128 * SEQ)),
            waits=[cc2], inc="dx", tag=f"s{u}_aorb{half}"))
    aout_ready = rb2[-1]

    def aout(kc):
        arena = t["a1b"] if kc < 8 else t["a2"]
        return arena[:, (kc % 8) * T:(kc % 8) * T + T]

    pre_o_tok = None
    if cfg.get("pre_o"):
        pre_o_tok = cfg["pre_o"](aout, aout_ready)

    last_nq = None
    for ft in range(KC):
        slot, queue = next_slot()
        wtok, base = cfg["make_wo"](ft, slot, queue, f"s{u}_wo{ft}")
        pr = ft % 2
        ps = ctx.ps(pr)
        mw = [wtok, aout_ready, pre_o_tok] + ctx.ps_wait(pr)
        tail = None
        closes = not cfg["has_extra_o"]
        for kc in range(KC):
            for hh in range(2):
                tail = p.add(
                    "tensor",
                    lambda e, kc=kc, hh=hh, ps=ps, base=base, closes=closes:
                    e.matmul(
                        ps[:, hh * 512:(hh + 1) * 512],
                        t["wbf"][:, base + kc * 128:base + (kc + 1) * 128],
                        aout(kc)[:, hh * 512:(hh + 1) * 512],
                        start=(kc == 0), stop=(closes and kc == KC - 1)),
                    waits=mw if kc == 0 and hh == 0 else [],
                    inc=True if closes and kc == KC - 1 and hh == 1 else None,
                    tag=f"s{u}_omm{ft}")
        etok = cfg["extra_mm_o"](ft, ps, f"s{u}_ox{ft}")
        ctx.buf_read(("wbf", ctx._cur_wslot), etok or tail)
        bias = cfg["bias_o"](ft)
        rft = t["R"][:, ft * T:(ft + 1) * T]
        r_tok = p.add("vector",
                      lambda e, ps=ps, rft=rft, bias=bias:
                      e.scalar_tensor_tensor(
                          rft, ps, bias if bias is not None else 0.0, rft,
                          ALU.add, ALU.add),
                      waits=[etok or tail], inc=True, tag=f"s{u}_racc{ft}")
        ctx.ps_read(pr, r_tok)
    ctx.buf_read("lnb", r_tok)
    ctx.resid_ready = r_tok
    return r_tok


def emit_attn_a(ctx):
    nc, p, t = ctx.nc, ctx.p, ctx.t
    u = ctx.nuid()
    lnb, a1b, a2, sq = t["lnb"], t["a1b"], t["a2"], t["sq"]
    trig, rsc = t["trig"], t["rsc"]
    norm_done = emit_ln(ctx)

    win_d = t["w_a0_in"].ap()
    wout_d = t["w_a0_out"].ap()
    trig_d = t["trig_a0"].ap()

    sqA = sq[:, 0:1024]
    sqB = sq[:, 1024:2048]
    cosT = trig[:, 0:1024]
    sinT = trig[:, 1024:2048]

    btok = p.add("sync", lambda e: e.dma_start(out=t["bias"][:, :],
                                               in_=t["b_a0"].ap()),
                 waits=ctx.buf_wait("bias"), inc="dx", tag=f"a{u}_bias")
    stok = p.add("sync", lambda e: e.dma_start(out=t["shufst"][:, :],
                                               in_=t["w_a0_shuf"].ap()),
                 waits=ctx.buf_wait("shufst"), inc="dx", tag=f"a{u}_shuf")
    sctok = p.add("gpsimd", lambda e: e.tensor_copy(t["shufbf"][:, :],
                                                    t["shufst"][:, :]),
                  waits=[stok], inc=True, tag=f"a{u}_shufc")
    ctx.buf_read("shufst", sctok)

    rot = [0]

    def proj_group(slab_ap, rhs_buf, extra_waits, tag):
        slot = rot[0] % 2
        queue = "sync" if slot == 0 else "scalar"
        ctok, base = stream_weight(ctx, slab_ap, slot, queue, tag)
        pr = rot[0] % 2
        rot[0] += 1
        ps = ctx.ps(pr)
        mw = [ctok] + extra_waits + ctx.ps_wait(pr)
        tail = None
        for kc in range(KC):
            for h in range(2):
                tail = p.add(
                    "tensor",
                    lambda e, kc=kc, h=h, ps=ps, base=base, rhs_buf=rhs_buf:
                    e.matmul(ps[:, h * 512:(h + 1) * 512],
                             t["wbf"][:, base + kc * 128:base + (kc + 1) * 128],
                             rhs_buf[:, kc * T + h * 512:kc * T + (h + 1) * 512],
                             start=(kc == 0), stop=(kc == KC - 1)),
                    waits=mw if kc == 0 and h == 0 else [],
                    inc=True if kc == KC - 1 and h == 1 else None,
                    tag=f"{tag}_mm")
        ctx.buf_read(("wbf", slot), tail)
        return pr, tail

    def rope(c, src, dst, evac_tok, pr, tag):
        ps = ctx.ps(pr)
        trig_w = ctx.buf_wait("trig")
        dtok = p.add("sync",
                     lambda e, c=c: e.dma_start(
                         out=t["trig"].ap().rearrange("p (k t) -> p k t", k=2),
                         in_=trig_d[:, c].rearrange("k p t -> p k t")),
                     waits=trig_w, inc="dwa", tag=f"{tag}_trig")
        sh_tail = None
        for h in range(2):
            sh_tail = p.add("tensor",
                            lambda e, h=h, ps=ps, src=src: e.matmul(
                                ps[:, h * 512:(h + 1) * 512], t["shufbf"][:, :],
                                src[:, h * 512:(h + 1) * 512],
                                start=True, stop=True),
                            waits=[evac_tok, sctok] if h == 0 else [],
                            inc=True if h == 1 else None, tag=f"{tag}_shuf")
        u1 = p.add("vector",
                   lambda e, ps=ps: e.tensor_tensor(rsc[:, :], ps, sinT,
                                                    ALU.mult),
                   waits=[sh_tail, dtok], inc=True, tag=f"{tag}_u1")
        ctx.ps_read(pr, u1)
        p.add("vector",
              lambda e, src=src, dst=dst: e.tensor_tensor(dst, src, cosT,
                                                          ALU.mult),
              tag=f"{tag}_u2")
        rtok = p.add("vector",
                     lambda e, dst=dst: e.tensor_tensor(dst, dst, rsc[:, :],
                                                        ALU.add),
                     inc=True, tag=f"{tag}_u3")
        ctx.buf_read("trig", rtok)
        return rtok

    s_k2 = t["PS"][0:1, 2048:3072]
    RNb = ctx.ps(2)
    bias = t["bias"]

    k2_tail = None
    w2 = ctx.ps_wait(2)
    for c in range(KC):
        ft = 16 + c
        dst = a2[:, c * T:(c + 1) * T]
        pr, mm = proj_group(win_d[ft], lnb,
                            [norm_done, btok] if c == 0 else [], f"a{u}_k{c}")
        if c < 8:
            ev = p.add("scalar",
                       lambda e, pr=pr, ft=ft: e.activation(
                           sqA, ctx.ps(pr), AF.Identity,
                           bias=bias[:, ft:ft + 1]),
                       waits=[mm] + ctx.buf_wait(("sq", 0)), inc=True,
                       tag=f"a{u}_kev{c}")
            ctx.ps_read(pr, ev)
            rtok = rope(c, sqA, dst, ev, pr, f"a{u}_kr{c}")
            ctx.buf_read(("sq", 0), rtok)
            ready = rtok
        else:
            ready = p.add("scalar",
                          lambda e, pr=pr, ft=ft, dst=dst: e.activation(
                              dst, ctx.ps(pr), AF.Identity,
                              bias=bias[:, ft:ft + 1]),
                          waits=[mm], inc=True, tag=f"a{u}_kev{c}")
            ctx.ps_read(pr, ready)
        sq_tok = p.add("scalar",
                       lambda e, dst=dst: e.activation(sqB, dst, AF.Square),
                       waits=[ready] + ctx.buf_wait(("sq", 1)), inc=True,
                       tag=f"a{u}_ksq{c}")
        ctx.buf_read(("a2sq", c), sq_tok)
        for h in range(2):
            k2_tail = p.add(
                "tensor",
                lambda e, h=h: e.matmul(
                    s_k2[:, h * 512:(h + 1) * 512], t["onec"][:, :],
                    sqB[:, h * 512:(h + 1) * 512],
                    start=(c == 0), stop=(c == KC - 1), skip_group_check=True),
                waits=([sq_tok] + (w2 if c == 0 else [])) if h == 0 else [],
                inc=True if h == 1 else None, tag=f"a{u}_k2_{c}_{h}")
        ctx.buf_read(("sq", 1), k2_tail)

    sr = t["strips"][0:1, 0:1024]
    sq_tok = p.add("scalar", lambda e: e.activation(sr, s_k2, AF.Sqrt),
                   waits=[k2_tail], inc=True, tag=f"a{u}_rn_sqrt")
    p.add("vector", lambda e: e.tensor_scalar_max(sr, sr, 1e-12),
          waits=[sq_tok], tag=f"a{u}_rn_max")
    rcp = p.add("vector", lambda e: e.reciprocal(sr, sr), inc=True,
                tag=f"a{u}_rn_rcp")
    rnb_tok = None
    for h in range(2):
        rnb_tok = p.add("tensor",
                        lambda e, h=h: e.matmul(
                            RNb[:, h * 512:(h + 1) * 512], t["oner"][:, :],
                            sr[:, h * 512:(h + 1) * 512],
                            start=True, stop=True),
                        waits=[rcp, sq_tok] if h == 0 else [],
                        inc=True if h == 1 else None, tag=f"a{u}_rnb{h}")

    ttr_tok = None
    for c in range(KC):
        ft = 32 + c
        kv = a2[:, c * T:(c + 1) * T]
        pr, mm = proj_group(win_d[ft], lnb, [], f"a{u}_v{c}")
        ev = p.add("scalar",
                   lambda e, pr=pr, ft=ft: e.activation(
                       sqA, ctx.ps(pr), AF.Identity, bias=bias[:, ft:ft + 1]),
                   waits=[mm] + ctx.buf_wait(("sq", 0)), inc=True,
                   tag=f"a{u}_vev{c}")
        ctx.ps_read(pr, ev)
        kv_tok = p.add("vector",
                       lambda e, kv=kv: e.tensor_tensor(kv, kv, sqA, ALU.mult),
                       waits=[ev] + ctx.buf_wait(("a2sq", c)), inc=True,
                       tag=f"a{u}_kv{c}")
        ctx.buf_read(("sq", 0), kv_tok)
        p.add("vector",
              lambda e, kv=kv: e.tensor_tensor(sqB, kv, RNb, ALU.mult),
              waits=[rnb_tok, kv_tok] + ctx.buf_wait(("sq", 1)),
              tag=f"a{u}_ttp{c}")
        ttr_tok = p.add("vector",
                        lambda e, c=c: e.reduce_sum(
                            t["amine"][:, c:c + 1], sqB, AX.X),
                        inc=True, tag=f"a{u}_ttr{c}")
        ctx.buf_read(("sq", 1), ttr_tok)
    ctx.ps_read(2, ttr_tok)

    ttr_tok = p.add("vector",
                    lambda e: e.tensor_copy(sqB[:, 0:64], sqB[:, 64:128]),
                    inc=True, tag=f"a{u}_fence")
    d1 = p.add("sync", lambda e: e.dma_start(out=t["cc_a_in"].ap(),
                                             in_=t["amine"][:, :]),
               waits=[ttr_tok], inc="dx", tag=f"a{u}_adma")
    cc = p.add("gpsimd",
               lambda e: e.collective_compute(
                   "AllReduce", ALU.add,
                   replica_groups=[[0, 1], [2, 3], [4, 5], [6, 7]],
                   ins=[t["cc_a_in"].ap().opt()],
                   outs=[t["cc_a_out"].ap().opt()]),
               waits=[d1], inc=CCSEM, tag=f"a{u}_cc")

    rope_toks = []
    for c in range(KC):
        dst = a1b[:, c * T:(c + 1) * T]
        pr, mm = proj_group(win_d[c], lnb, [], f"a{u}_q{c}")
        if c < 8:
            ev = p.add("scalar",
                       lambda e, pr=pr, c=c: e.activation(
                           sqA, ctx.ps(pr), AF.Identity, bias=bias[:, c:c + 1]),
                       waits=[mm] + ctx.buf_wait(("sq", 0)), inc=True,
                       tag=f"a{u}_qev{c}")
            ctx.ps_read(pr, ev)
            rtok = rope(c, sqA, dst, ev, pr, f"a{u}_qr{c}")
            ctx.buf_read(("sq", 0), rtok)
            rope_toks.append(rtok)
        else:
            ev = p.add("scalar",
                       lambda e, pr=pr, c=c, dst=dst: e.activation(
                           dst, ctx.ps(pr), AF.Identity, bias=bias[:, c:c + 1]),
                       waits=[mm], inc=True, tag=f"a{u}_qev{c}")
            ctx.ps_read(pr, ev)
            rope_toks.append(ev)

    d2 = p.add("sync", lambda e: e.dma_start(out=t["afull"][:, :],
                                             in_=t["cc_a_out"].ap()),
               waits=[cc], inc="dx", tag=f"a{u}_ardma")
    y_toks = []
    for c in range(KC):
        y_toks.append(p.add(
            "scalar",
            lambda e, c=c: e.activation(a1b[:, c * T:(c + 1) * T],
                                        a1b[:, c * T:(c + 1) * T],
                                        AF.Identity, scale=t["afull"][:, c:c + 1]),
            waits=[d2, rope_toks[c]], inc=True, tag=f"a{u}_y{c}"))

    r_tok = None
    for ft in range(KC):
        slot = rot[0] % 2
        queue = "sync" if slot == 0 else "scalar"
        ctok, base = stream_weight(ctx, wout_d[ft], slot, queue, f"a{u}_o{ft}")
        pr = rot[0] % 2
        rot[0] += 1
        ps = ctx.ps(pr)
        mw = [ctok, y_toks[-1]] + ctx.ps_wait(pr)
        tail = None
        for kc in range(KC):
            for h in range(2):
                tail = p.add(
                    "tensor",
                    lambda e, kc=kc, h=h, ps=ps, base=base: e.matmul(
                        ps[:, h * 512:(h + 1) * 512],
                        t["wbf"][:, base + kc * 128:base + (kc + 1) * 128],
                        a1b[:, kc * T + h * 512:kc * T + (h + 1) * 512],
                        start=(kc == 0), stop=(kc == KC - 1)),
                    waits=mw if kc == 0 and h == 0 else [],
                    inc=True if kc == KC - 1 and h == 1 else None,
                    tag=f"a{u}_omm{ft}")
        ctx.buf_read(("wbf", slot), tail)
        rft = t["R"][:, ft * T:(ft + 1) * T]
        r1 = p.add("vector",
                   lambda e, ps=ps, ft=ft, rft=rft: e.scalar_tensor_tensor(
                       rft, ps, bias[:, 48 + ft:48 + ft + 1], rft,
                       ALU.add, ALU.add),
                   waits=[tail], inc=True, tag=f"a{u}_racc{ft}")
        ctx.ps_read(pr, r1)
        r_tok = p.add("vector",
                      lambda e, ft=ft, rft=rft: e.tensor_tensor(
                          rft, rft, lnb[:, ft * T:(ft + 1) * T], ALU.add),
                      inc=True, tag=f"a{u}_rln{ft}")
    ctx.buf_read("lnb", r_tok)
    ctx.buf_read("bias", r_tok)
    ctx.resid_ready = r_tok
    return r_tok


BLOCK_EMITTERS = {}


def build_program(blocks):
    nc = bass.Bass("TRN2", target_bir_lowering=False,
                   dynamic_dma_scratch_size=2048)
    plan = Planner()
    ctx = Ctx(nc, plan)
    names = declare_ios(ctx, blocks)
    has_attn = any(b in blocks for b in ATTN_BLOCKS)
    with ExitStack() as es:
        alloc_onchip(ctx, es)
        if has_attn:
            ctx.roff_reg = es.enter_context(nc.sync.register("roff"))
            ctx.roff2_reg = es.enter_context(nc.sync.register("roff2"))
        sems = {}
        for s in plan.semv:
            sems[s] = es.enter_context(nc.semaphore(s))
        blk = es.enter_context(nc.Block())

        emit_setup(ctx)
        if has_attn:
            emit_attn_consts(ctx)
        for b in blocks:
            BLOCK_EMITTERS[b](ctx)
        emit_store(ctx)

        for eng in ENGS:
            def body(h, eng=eng):
                for fn, waits, (sem, amt), tag in plan.ops[eng]:
                    for s, v in waits:
                        h.wait_ge(sems[s], v)
                    ins = fn(h)
                    if sem is not None:
                        ins.then_inc(sems[sem], amt)
            getattr(blk, eng)(body)
    return nc, names


def emit_attn_a1(ctx):
    p, t = ctx.p, ctx.t
    bias_tok = p.add("sync",
                     lambda e: e.dma_start(out=t["bias"][:, :],
                                           in_=t["b_a4"].ap()),
                     waits=ctx.buf_wait("bias"), inc="dx", tag="a4_bias")
    sc = 1.0 / math.sqrt(128.0)
    bias_tok = p.add("scalar",
                     lambda e: e.activation(t["bias"][:, 0:8],
                                            t["bias"][:, 0:8],
                                            AF.Identity, scale=sc),
                     waits=[bias_tok], inc=True, tag="a4_biassc")
    wq = t["w_a4_qkv"].ap()
    wo = t["w_a4_o"].ap()
    cfg = dict(
        name="a4", causal=True, has_extra_w=False, has_extra_o=False,
        scale=sc,
        bias_tok=bias_tok,
        make_w=lambda h, w, slot, queue, tag:
            stream_weight(ctx, wq[h, w], slot, queue, tag,
                          scale=sc if w == 0 else None),
        extra_mm=lambda h, w, ps, tag: None,
        bias_col=lambda w, h: t["bias"][:, w * 8 + h:w * 8 + h + 1],
        make_wo=lambda ft, slot, queue, tag:
            stream_weight(ctx, wo[ft], slot, queue, tag),
        extra_mm_o=lambda ft, ps, tag: None,
        bias_o=lambda ft: t["bias"][:, 48 + ft:48 + ft + 1],
    )
    rt = emit_softmax_attn(ctx, cfg)
    ctx.buf_read("bias", rt)
    return rt


def emit_ffn1(ctx):
    nc, p, t = ctx.nc, ctx.p, ctx.t
    u = ctx.nuid()
    lnb, a1b, a2, sq, suf = t["lnb"], t["a1b"], t["a2"], t["sq"], t["suf"]
    norm_done = emit_ln(ctx)

    sd = p.add("sync", lambda e: e.dma_start(out=suf[:, :],
                                             in_=t["v_f1"].ap()),
               waits=ctx.buf_wait("suf"), inc="dx", tag=f"f1{u}_suf")
    s1_ = p.add("scalar",
                lambda e: e.activation(suf[:, 80:208], suf[:, 80:208],
                                       AF.Identity,
                                       scale=1.0 / math.sqrt(2048.0)),
                waits=[sd], inc=True, tag=f"f1{u}_sc1")
    suf_tok = p.add("scalar",
                    lambda e: e.activation(suf[:, 208:224], suf[:, 208:224],
                                           AF.Identity,
                                           scale=1.0 / math.sqrt(8192.0)),
                    inc=True, tag=f"f1{u}_sc2")
    hd_ = p.add("sync", lambda e: e.dma_start(out=t["shufst"][:, :],
                                              in_=t["h128"].ap()),
                waits=ctx.buf_wait("shufst"), inc="dx", tag=f"f1{u}_h128")
    h128_tok = p.add("gpsimd",
                     lambda e: e.tensor_copy(t["h128bf"][:, :],
                                             t["shufst"][:, :]),
                     waits=[hd_], inc=True, tag=f"f1{u}_h128c")
    ctx.buf_read("shufst", h128_tok)

    su_tok = None
    for c in range(KC):
        su_tok = p.add("gpsimd",
                       lambda e, c=c: e.tensor_scalar_mul(
                           lnb[:, c * T:(c + 1) * T],
                           lnb[:, c * T:(c + 1) * T], suf[:, c:c + 1]),
                       waits=[norm_done, suf_tok] if c == 0 else [],
                       inc=True if c == KC - 1 else None, tag=f"f1{u}_su{c}")

    rotc = [0]

    def rot_pair():
        pr = rotc[0] % 3
        rotc[0] += 1
        return pr

    ev1 = {}
    for c in range(KC):
        pr = rot_pair()
        ps = ctx.ps(pr)
        mw = [su_tok, h128_tok] + ctx.ps_wait(pr)
        tail = None
        for hh in range(2):
            tail = p.add("tensor",
                         lambda e, c=c, hh=hh, ps=ps: e.matmul(
                             ps[:, hh * 512:(hh + 1) * 512],
                             t["h128bf"][:, :],
                             lnb[:, c * T + hh * 512:c * T + (hh + 1) * 512],
                             start=True, stop=True),
                         waits=mw if hh == 0 else [],
                         inc=True if hh == 1 else None, tag=f"f1{u}_w1{c}")
        ev1[c] = p.add("scalar",
                       lambda e, c=c, ps=ps: e.activation(
                           a1b[:, c * T:(c + 1) * T], ps, AF.Identity),
                       waits=[tail], inc=True, tag=f"f1{u}_e1{c}")
        ctx.ps_read(pr, ev1[c])

    bufs = [a1b, lnb]
    toks = dict(ev1)
    for st, s in enumerate([1, 2, 4, 8]):
        src = bufs[st % 2]
        dst = bufs[(st + 1) % 2]
        ntoks = {}
        for j in range(0, KC, 2 * s):
            for k in range(s):
                a, b = j + k, j + k + s
                wa = [toks[a], toks[b]]
                t1 = p.add("vector",
                           lambda e, a=a, b=b, src=src, dst=dst:
                           e.tensor_tensor(dst[:, a * T:(a + 1) * T],
                                           src[:, a * T:(a + 1) * T],
                                           src[:, b * T:(b + 1) * T],
                                           ALU.add),
                           waits=wa, tag=f"f1{u}_b{st}a{a}")
                t2 = p.add("vector",
                           lambda e, a=a, b=b, src=src, dst=dst:
                           e.tensor_tensor(dst[:, b * T:(b + 1) * T],
                                           src[:, a * T:(a + 1) * T],
                                           src[:, b * T:(b + 1) * T],
                                           ALU.subtract),
                           inc=True, tag=f"f1{u}_b{st}b{b}")
                ntoks[a] = t2
                ntoks[b] = t2
        toks = ntoks
    x1_tok = publish_dve(ctx, f"f1{u}_x1pub")

    def g_ap(hc, c0, c1):
        base = a2 if hc < 32 else lnb
        off = (hc % 32) * 512
        return base[:, off + c0:off + c1]

    if DBG_G == "x1":
        t["gdump"] = nc.dram_tensor("gdump", [16, 128, T], BF16,
                                    kind="ExternalOutput")
        dd = p.add("sync", lambda e: e.dma_start(
            out=t["gdump"].ap().rearrange("k p t -> p k t"),
            in_=a1b.ap().rearrange("p (k t) -> p k t", k=16)),
            waits=[x1_tok], inc="dx", tag="x1dump")
        p.add("sync", lambda e: e.nop(), waits=[dd])

    wg_d = t["w_f1_gate"].ap()
    wu_d = t["w_f1_up"].ap()
    wd_d = t["w_f1_down"].ap()
    wbf = t["wbf"]
    r_tok = None

    for th in range(2):
        tc = th * 512
        g_toks = []
        for hc in range(HC):
            cg, bg = stream_weight(ctx, wg_d[hc], 0, "sync", f"f1{u}g{th}_{hc}", cast_eng="gpsimd")
            cu, bu = stream_weight(ctx, wu_d[hc], 1, "scalar",
                                   f"f1{u}u{th}_{hc}", cast_eng="vector")
            pg = rot_pair()
            pu = rot_pair()
            gate_ps = ctx.ps(pg, 0, 512)
            up_ps = ctx.ps(pu, 0, 512)
            mwg = [cg, x1_tok] + ctx.ps_wait(pg)
            mwu = [cu] + ctx.ps_wait(pu)
            gt = ut = None
            for kc in range(KC):
                gt = p.add("tensor",
                           lambda e, kc=kc, bg=bg, gate_ps=gate_ps, tc=tc:
                           e.matmul(gate_ps,
                                    wbf[:, bg + kc * 128:bg + (kc + 1) * 128],
                                    a1b[:, kc * T + tc:kc * T + tc + 512],
                                    start=(kc == 0), stop=(kc == KC - 1)),
                           waits=mwg if kc == 0 else [],
                           inc=True if kc == KC - 1 else None,
                           tag=f"f1{u}_gmm{th}_{hc}")
            for kc in range(KC):
                ut = p.add("tensor",
                           lambda e, kc=kc, bu=bu, up_ps=up_ps, tc=tc:
                           e.matmul(up_ps,
                                    wbf[:, bu + kc * 128:bu + (kc + 1) * 128],
                                    a1b[:, kc * T + tc:kc * T + tc + 512],
                                    start=(kc == 0), stop=(kc == KC - 1)),
                           waits=mwu if kc == 0 else [],
                           inc=True if kc == KC - 1 else None,
                           tag=f"f1{u}_umm{th}_{hc}")
            ctx.buf_read(("wbf", 0), gt)
            ctx.buf_read(("wbf", 1), ut)
            sg = sq[:, 0:512]
            ub = sq[:, 1024:1536]
            silu_tok = p.add("scalar",
                            lambda e, sg=sg, gate_ps=gate_ps, hc=hc:
                            e.activation(sg, gate_ps, AF.Silu,
                                         scale=suf[:, 144 + hc:145 + hc]),
                            waits=[gt] + ctx.buf_wait(("sq", 0)), inc=True,
                            tag=f"f1{u}_si{th}_{hc}")
            ctx.ps_read(pg, silu_tok)
            up_tok = p.add("scalar",
                           lambda e, ub=ub, up_ps=up_ps, hc=hc:
                           e.activation(ub, up_ps, AF.Identity,
                                        scale=suf[:, 80 + hc:81 + hc]),
                           waits=[ut] + ctx.buf_wait(("sq", 2)), inc=True,
                           tag=f"f1{u}_ue{th}_{hc}")
            ctx.ps_read(pu, up_tok)
            g_tok = p.add("vector",
                          lambda e, hc=hc, sg=sg, ub=ub:
                          e.scalar_tensor_tensor(
                              g_ap(hc, 0, 512), sg, suf[:, 16 + hc:17 + hc],
                              ub, ALU.mult, ALU.mult),
                          waits=[silu_tok, up_tok] +
                                ctx.buf_wait(("g1", hc)),
                          inc=True, tag=f"f1{u}_g{th}_{hc}")
            ctx.buf_read(("sq", 0), g_tok)
            ctx.buf_read(("sq", 2), g_tok)
            g_toks.append(publish_dve(ctx, f"f1{u}_gp{th}_{hc}"))

        if DBG_G == f"gA{th}":
            t["gdump"] = nc.dram_tensor("gdump", [64, 128, 512], BF16,
                                        kind="ExternalOutput")
            dds = []
            for half2 in range(2):
                arena = a2 if half2 == 0 else lnb
                dds.append(p.add("sync", lambda e, half2=half2, arena=arena:
                    e.dma_start(
                        out=t["gdump"].ap()[half2 * 32:(half2 + 1) * 32]
                            .rearrange("k p t -> p k t"),
                        in_=arena.ap()[:, 0:32 * 512]
                            .rearrange("p (k t) -> p k t", k=32)),
                    waits=[g_toks[-1]], inc="dx", tag=f"gdump{half2}"))
            p.add("sync", lambda e: e.nop(), waits=[dds[-1]])
            ctx._dbg_tok = dds[-1]

        z_toks = {}
        for hc in range(HC):
            pr = rot_pair()
            ps = ctx.ps(pr, 0, 512)
            mm = p.add("tensor",
                       lambda e, hc=hc, ps=ps: e.matmul(
                           ps, t["h128bf"][:, :], g_ap(hc, 0, 512),
                           start=True, stop=True),
                       waits=[g_toks[hc], getattr(ctx, "_dbg_tok", None)]
                             + ctx.ps_wait(pr),
                       inc=True, tag=f"f1{u}_z{th}_{hc}")
            z_toks[hc] = p.add("scalar",
                               lambda e, hc=hc, ps=ps: e.activation(
                                   g_ap(hc, 0, 512), ps, AF.Identity),
                               waits=[mm], inc=True, tag=f"f1{u}_ze{th}_{hc}")
            ctx.ps_read(pr, z_toks[hc])
        toks2 = dict(z_toks)
        for st, s in enumerate([1, 2, 4, 8, 16, 32]):
            ntoks = {}
            pi = 0
            for j in range(0, HC, 2 * s):
                for k in range(s):
                    a, b = j + k, j + k + s
                    eng = "vector" if pi % 2 == 0 else "gpsimd"
                    tmp = (t["rsc"][:, 0:512] if eng == "vector"
                           else t["rsc"][:, 512:1024])
                    wa = [toks2[a], toks2[b]]
                    if st > 0:
                        wa += [pv for pv in stage_bar]
                    p.add(eng,
                          lambda e, a=a, b=b, tmp=tmp: e.tensor_tensor(
                              tmp, g_ap(a, 0, 512), g_ap(b, 0, 512),
                              ALU.subtract),
                          waits=wa, tag=f"f1{u}_c{st}s{a}")
                    p.add(eng,
                          lambda e, a=a, b=b: e.tensor_tensor(
                              g_ap(a, 0, 512), g_ap(a, 0, 512),
                              g_ap(b, 0, 512), ALU.add),
                          tag=f"f1{u}_c{st}a{a}")
                    t2 = p.add(eng,
                               lambda e, b=b, tmp=tmp: e.tensor_copy(
                                   g_ap(b, 0, 512), tmp),
                               inc=True, tag=f"f1{u}_c{st}b{b}")
                    ntoks[a] = t2
                    ntoks[b] = t2
                    pi += 1
            stage_bar = [v for v in
                         {tk[0]: tk for tk in ntoks.values()}.values()]
            toks2 = ntoks
        g_done = stage_bar + [publish_dve(ctx, f"f1{u}_gdpub{th}")]
        if DBG_G == f"gB{th}":
            t["gdump"] = nc.dram_tensor("gdump", [64, 128, 512], BF16,
                                        kind="ExternalOutput")
            dds = []
            for half2 in range(2):
                arena = a2 if half2 == 0 else lnb
                dds.append(p.add("sync", lambda e, half2=half2, arena=arena:
                    e.dma_start(
                        out=t["gdump"].ap()[half2 * 32:(half2 + 1) * 32]
                            .rearrange("k p t -> p k t"),
                        in_=arena.ap()[:, 0:32 * 512]
                            .rearrange("p (k t) -> p k t", k=32)),
                    waits=g_done, inc="dx", tag=f"gbdump{half2}"))
            p.add("sync", lambda e: e.nop(), waits=[dds[-1]])
            g_done = g_done + [dds[-1]]

        for ft in range(KC):
            pr = rot_pair()
            dn_ps = ctx.ps(pr, 0, 512)
            mm_tail = None
            mw0 = ctx.ps_wait(pr) + g_done
            for sl in range(4):
                cd, bd = stream_weight(ctx, wd_d[ft, sl],
                                       sl % 2, "sync" if sl % 2 == 0
                                       else "scalar",
                                       f"f1{u}d{th}_{ft}_{sl}",
                                       cast_eng="gpsimd")
                for i in range(16):
                    gi = sl * 16 + i
                    mm_tail = p.add(
                        "tensor",
                        lambda e, i=i, gi=gi, bd=bd, dn_ps=dn_ps, sl=sl:
                        e.matmul(dn_ps,
                                 wbf[:, bd + i * 128:bd + (i + 1) * 128],
                                 g_ap(gi, 0, 512),
                                 start=(sl == 0 and i == 0),
                                 stop=(sl == 3 and i == 15)),
                        waits=([cd] + (mw0 if sl == 0 else []))
                              if i == 0 else [],
                        inc=True if sl == 3 and i == 15 else None,
                        tag=f"f1{u}_dmm{th}_{ft}")
                ctx.buf_read(("wbf", sl % 2), mm_tail)
            rft = t["R"][:, ft * T + tc:ft * T + tc + 512]
            r_tok = p.add("vector",
                          lambda e, ft=ft, dn_ps=dn_ps, rft=rft:
                          e.scalar_tensor_tensor(
                              rft, dn_ps, suf[:, 208 + ft:209 + ft], rft,
                              ALU.mult, ALU.add),
                          waits=[mm_tail], inc=True,
                          tag=f"f1{u}_racc{th}_{ft}")
            ctx.ps_read(pr, r_tok)
        if th == 0:
            for hc in range(HC):
                ctx.buf_read(("g1", hc), r_tok)
    ctx.buf_read("lnb", r_tok)
    ctx.buf_read("suf", r_tok)
    ctx.resid_ready = r_tok
    return r_tok


def emit_attn_b0(ctx):
    nc, p, t = ctx.nc, ctx.p, ctx.t
    sc = 1.0 / math.sqrt(128.0)
    bias_tok = p.add("sync",
                     lambda e: e.dma_start(out=t["bias"][:, :],
                                           in_=t["b_a2"].ap()),
                     waits=ctx.buf_wait("bias"), inc="dx", tag="a2_bias")
    bias_tok = p.add("scalar",
                     lambda e: e.activation(t["bias"][:, 0:8],
                                            t["bias"][:, 0:8],
                                            AF.Identity, scale=sc),
                     waits=[bias_tok], inc=True, tag="a2_biassc")
    bd = p.add("sync",
               lambda e: e.dma_start(out=t["wst"][:, 0:1024],
                                     in_=t["w_a2_B"].ap()),
               waits=ctx.buf_wait(("wst", 0)), inc="dwa", tag="a2_bt")
    bc = p.add("gpsimd",
               lambda e: e.tensor_copy(t["blor"][:, :], t["wst"][:, 0:1024]),
               waits=[bd] + ctx.buf_wait("blor"), inc=True, tag="a2_btc")
    ctx.buf_read(("wst", 0), bc)
    wq = t["w_a2_qkv"].ap()
    wo = t["w_a2_o"].ap()
    ul = t["ul"]
    SCp = t["PS"]

    def pre_heads(lnf, lnf_ready):
        ev = None
        for w in range(3):
            up = SCp[0:16, 0:2048]
            mw = [bc, lnf_ready] + ctx.ps_wait(0) + ctx.ps_wait(1)
            tail = None
            for kc in range(KC):
                for q4 in range(4):
                    tail = p.add(
                        "tensor",
                        lambda e, kc=kc, q4=q4, w=w: e.matmul(
                            up[:, q4 * 512:(q4 + 1) * 512],
                            t["blor"][:, (w * KC + kc) * 16:
                                      (w * KC + kc) * 16 + 16],
                            lnf(kc)[:, q4 * 512:(q4 + 1) * 512],
                            start=(kc == 0), stop=(kc == KC - 1)),
                        waits=mw if kc == 0 and q4 == 0 else [],
                        inc=True if kc == KC - 1 and q4 == 3 else None,
                        tag=f"a2_u{w}")
            s = 0.5 * sc if w == 0 else 0.5
            ev = p.add("scalar",
                       lambda e, w=w, s=s: e.activation(
                           ul[w * 32:w * 32 + 16, :], up, AF.Identity,
                           scale=s),
                       waits=[tail] + (ctx.buf_wait("ul") if w == 0 else []),
                       inc=True, tag=f"a2_ue{w}")
            ctx.ps_read(0, ev)
            ctx.ps_read(1, ev)
        return ev

    def extra_mm(h, w, ps, tag):
        tok = None
        base = ctx._cur_wbase
        for q4 in range(4):
            tok = p.add("tensor",
                        lambda e, q4=q4, w=w, ps=ps, base=base: e.matmul(
                            ps[:, q4 * 512:(q4 + 1) * 512],
                            t["wbf"][w * 32:w * 32 + 16,
                                     base + 2048:base + 2176],
                            ul[w * 32:w * 32 + 16,
                               q4 * 512:(q4 + 1) * 512],
                            start=False, stop=True),
                        inc=True if q4 == 3 else None, tag=tag)
        return tok

    def make_w(h, w, slot, queue, tag):
        tok, base = stream_weight(ctx, wq[h, w], slot, queue, tag,
                                  scale=sc if w == 0 else None, width=2176)
        ctx._cur_wbase = base
        return tok, base

    def pre_o(aout, aout_ready):
        up = SCp[0:16, 0:1024]
        mw = [aout_ready] + ctx.ps_wait(0) + ctx.buf_wait("ul")
        tail = None
        for kc in range(KC):
            for hh in range(2):
                tail = p.add(
                    "tensor",
                    lambda e, kc=kc, hh=hh: e.matmul(
                        up[:, hh * 512:(hh + 1) * 512],
                        t["blor"][:, 768 + kc * 16:768 + kc * 16 + 16],
                        aout(kc)[:, hh * 512:(hh + 1) * 512],
                        start=(kc == 0), stop=(kc == KC - 1)),
                    waits=mw if kc == 0 and hh == 0 else [],
                    inc=True if kc == KC - 1 and hh == 1 else None,
                    tag="a2_uo")
        ev = p.add("scalar",
                   lambda e: e.activation(ul[0:16, 0:1024], up, AF.Identity,
                                          scale=0.5),
                   waits=[tail], inc=True, tag="a2_uoe")
        ctx.ps_read(0, ev)
        return ev

    def extra_mm_o(ft, ps, tag):
        tok = None
        base = ctx._cur_wbase
        for hh in range(2):
            tok = p.add("tensor",
                        lambda e, hh=hh, ps=ps, base=base: e.matmul(
                            ps[:, hh * 512:(hh + 1) * 512],
                            t["wbf"][0:16, base + 2048:base + 2176],
                            ul[0:16, hh * 512:(hh + 1) * 512],
                            start=False, stop=True),
                        inc=True if hh == 1 else None, tag=tag)
        return tok

    def make_wo(ft, slot, queue, tag):
        tok, base = stream_weight(ctx, wo[ft], slot, queue, tag, width=2176)
        ctx._cur_wbase = base
        return tok, base

    cfg = dict(
        name="a2", causal=False, has_extra_w=True, has_extra_o=True,
        scale=sc, bias_tok=bias_tok,
        make_w=make_w, extra_mm=extra_mm,
        bias_col=lambda w, h: t["bias"][:, w * 8 + h:w * 8 + h + 1],
        make_wo=make_wo, extra_mm_o=extra_mm_o,
        bias_o=lambda ft: t["bias"][:, 48 + ft:48 + ft + 1],
        pre_heads=pre_heads, pre_o=pre_o,
    )
    rt = emit_softmax_attn(ctx, cfg)
    ctx.buf_read("bias", rt)
    ctx.buf_read("blor", rt)
    ctx.buf_read("ul", rt)
    return rt


def emit_attn_b1(ctx):
    nc, p, t = ctx.nc, ctx.p, ctx.t
    sc = 1.0 / math.sqrt(128.0)
    bias_tok = p.add("sync",
                     lambda e: e.dma_start(out=t["bias"][:, :],
                                           in_=t["b_a6"].ap()),
                     waits=ctx.buf_wait("bias"), inc="dx", tag="a6_bias")
    bias_tok = p.add("scalar",
                     lambda e: e.activation(t["bias"][:, 0:8],
                                            t["bias"][:, 0:8],
                                            AF.Identity, scale=sc),
                     waits=[bias_tok], inc=True, tag="a6_biassc")
    qc = t["w_a6_qkv_c"].ap()
    qs = t["w_a6_qkv_s"].ap()
    oc = t["w_a6_o_c"].ap()
    os_ = t["w_a6_o_s"].ap()
    wbf = t["wbf"]

    def dequant(ca, cb, tag):
        dq = p.add("gpsimd",
                   lambda e: e.tensor_tensor(
                       wbf[:, 0:2048], wbf[:, 0:2048],
                       wbf[:, SLOTW:SLOTW + 2048], ALU.mult),
                   waits=[ca, cb], inc=True, tag=tag)
        ctx.buf_read(("wbf", 1), dq)
        ctx._cur_wslot = 0
        return dq

    def make_w(h, w, slot, queue, tag):
        ca, _ = stream_weight(ctx, qc[h, w], 0, "sync", tag + "c",
                              scale=sc if w == 0 else None)
        cb, _ = stream_weight(ctx, qs[h, w], 1, "scalar", tag + "s")
        return dequant(ca, cb, tag + "dq"), 0

    def make_wo(ft, slot, queue, tag):
        ca, _ = stream_weight(ctx, oc[ft], 0, "sync", tag + "c")
        cb, _ = stream_weight(ctx, os_[ft], 1, "scalar", tag + "s")
        return dequant(ca, cb, tag + "dq"), 0

    cfg = dict(
        name="a6", causal=False, has_extra_w=False, has_extra_o=False,
        scale=sc, bias_tok=bias_tok,
        make_w=make_w, extra_mm=lambda h, w, ps, tag: None,
        bias_col=lambda w, h: t["bias"][:, w * 8 + h:w * 8 + h + 1],
        make_wo=make_wo, extra_mm_o=lambda ft, ps, tag: None,
        bias_o=lambda ft: t["bias"][:, 48 + ft:48 + ft + 1],
    )
    rt = emit_softmax_attn(ctx, cfg)
    ctx.buf_read("bias", rt)
    return rt


BLOCK_EMITTERS[0] = emit_attn_a
BLOCK_EMITTERS[1] = emit_ffn0
BLOCK_EMITTERS[2] = emit_attn_b0
BLOCK_EMITTERS[3] = emit_ffn0
BLOCK_EMITTERS[4] = emit_attn_a1
BLOCK_EMITTERS[5] = emit_ffn1
BLOCK_EMITTERS[6] = emit_attn_b1
BLOCK_EMITTERS[7] = emit_ffn1



def _run_launches(x_fm_per_core, pre, launch_plan):
    state = x_fm_per_core
    for blocks in launch_plan:
        nc, names = build_program(blocks)
        in_maps = []
        for c in range(NCORES):
            m = {"rin": state[c]}
            for n in names:
                m[n] = pre[n][c]
            in_maps.append(m)
        res = run_bass_kernel_spmd(nc, in_maps, core_ids=list(range(NCORES)))
        state = [np.asarray(r["rout"]) for r in res.results]
    return state


def _to_fm(x):
    out = []
    for c in range(NCORES):
        sl = x[c * T:(c + 1) * T]
        out.append(np.ascontiguousarray(
            sl.T.reshape(KC, 128, T).astype(np.float32)))
    return out


def _from_fm(state):
    x = np.empty((NCORES * T, D), np.float32)
    for c in range(NCORES):
        x[c * T:(c + 1) * T] = state[c].reshape(D, T).T
    return x


def kernel(**inp):
    x = np.asarray(inp["L0_attn_a_X"], np.float32).reshape(NCORES * T, D)
    blocks_all = [b for launch in LAUNCH_PLAN for b in launch]
    pre = host_prep(inp, blocks_all)
    state = _run_launches(_to_fm(x), pre, LAUNCH_PLAN)
    return _from_fm(state).reshape(4, SEQ, D)
